# version 03
# baseline (speedup 1.0000x reference)
import numpy as np
import ml_dtypes
from collections import defaultdict
from contextlib import ExitStack

import os
import concourse.bass as bass
import concourse.mybir as mybir
from concourse.bass_utils import run_bass_kernel_spmd

F32 = mybir.dt.float32
BF16 = mybir.dt.bfloat16
AF = mybir.ActivationFunctionType
ALU = mybir.AluOpType
AX = mybir.AxisListType
BF = ml_dtypes.bfloat16

ENGINES = ("tensor", "vector", "scalar", "gpsimd", "sync")

D = 2048
S = 2048
H = 16
DH = 128
DFF = 8192
NCK = 16
BL = 2
TOK = BL * S



class Buf:
    __slots__ = ("name", "writer", "readers")

    def __init__(self, name):
        self.name = name
        self.writer = None
        self.readers = []


class DmaSem:
    __slots__ = ("name", "count", "handle", "final_wait")

    def __init__(self, name, final_wait=False):
        self.name = name
        self.count = 0
        self.handle = None
        self.final_wait = final_wait


class _Op:
    __slots__ = ("engine", "emit_fn", "reads", "writes", "dma_sem", "dma_cum",
                 "inc_needed", "inc_count", "waits", "tag", "extra_dma_waits",
                 "barrier_state")

    def __init__(self, engine, emit_fn, reads, writes, dma_sem, tag):
        self.engine = engine
        self.emit_fn = emit_fn
        self.reads = list(reads)
        self.writes = list(writes)
        self.dma_sem = dma_sem
        self.dma_cum = 0
        self.inc_needed = False
        self.inc_count = 0
        self.waits = []
        self.tag = tag
        self.extra_dma_waits = None
        self.barrier_state = None


class Prog:
    def __init__(self):
        self.ops = []
        self.dma_sems = {}
        self._bufs = {}
        self._last_op = {}

    def buf(self, name):
        b = self._bufs.get(name)
        if b is None:
            b = self._bufs[name] = Buf(name)
        return b

    def dma_sem(self, name):
        s = self.dma_sems.get(name)
        if s is None:
            s = self.dma_sems[name] = DmaSem(
                name, final_wait=name.startswith(("cst", "prep")))
        return s

    def op(self, engine, emit_fn, reads=(), writes=(), dma_sem=None, tag=""):
        o = _Op(engine, emit_fn, reads, writes, dma_sem, tag)
        if dma_sem is not None:
            dma_sem.count += 16
            o.dma_cum = dma_sem.count
        self.ops.append(o)
        self._last_op[engine] = o
        return o

    def dma(self, engine, sem_name, emit_fn, reads=(), writes=(), tag=""):
        return self.op(engine, emit_fn, reads, writes, self.dma_sem(sem_name), tag)

    def barrier(self, tag="bar"):
        o = _Op("sync", None, (), (), None, f"{tag}S")
        o.barrier_state = (dict(self._last_op),
                           [(sm, sm.count) for sm in self.dma_sems.values()])
        self.ops.append(o)

    def plan(self):
        for b in self._bufs.values():
            b.writer = None
            b.readers = []
        deps = [set() for _ in self.ops]
        idx = {}
        for i, o in enumerate(self.ops):
            idx[id(o)] = i
        for i, o in enumerate(self.ops):
            if o.barrier_state is not None:
                for b in self._bufs.values():
                    b.writer = None
                    b.readers = []
                continue
            for b in o.reads:
                w = b.writer
                if w is not None and w is not o:
                    deps[i].add(w)
            for b in o.writes:
                w = b.writer
                if w is not None and w is not o:
                    same_group = (o.dma_sem is not None
                                  and w.dma_sem is o.dma_sem and not b.readers)
                    if not same_group:
                        deps[i].add(w)
                for r in b.readers:
                    if r is not o:
                        deps[i].add(r)
            for b in o.reads:
                b.readers.append(o)
            for b in o.writes:
                b.writer = o
                b.readers = []

        eng_ops = defaultdict(list)
        for o in self.ops:
            if o.barrier_state is None:
                eng_ops[o.engine].append(o)

        op_knowledge = {}
        eng_know = {e: {} for e in ENGINES}
        wait_edges = [[] for _ in self.ops]
        cur_barrier = None
        for i, o in enumerate(self.ops):
            if o.barrier_state is not None:
                cur_barrier = o.barrier_state
                continue
            know = eng_know[o.engine]
            best = {}
            dd = set(deps[i])
            if cur_barrier is not None:
                last_ops, _ = cur_barrier
                for e2, w2 in last_ops.items():
                    if w2 is not None:
                        dd.add(w2)
            for w in sorted(dd, key=lambda x: idx[id(x)]):
                if w.engine == "tensor" and o.engine == "tensor" and w.dma_sem is None:
                    continue
                if w.dma_sem is not None:
                    key = ("dma", w.dma_sem.name)
                    val = w.dma_sem.count if w.dma_sem.final_wait else w.dma_cum
                    cur = know.get(key)
                    if cur is not None and cur >= val:
                        continue
                    if key not in best or best[key][0] < val:
                        best[key] = (val, w)
                    know[key] = val
                else:
                    key = ("eng", w.engine)
                    wpos = idx[id(w)]
                    cur = know.get(key)
                    if cur is not None and cur >= wpos:
                        continue
                    if key not in best or best[key][0] < wpos:
                        best[key] = (wpos, w)
                    know[key] = wpos
                pk = op_knowledge.get(id(w))
                if pk:
                    for k2, v2 in pk.items():
                        if know.get(k2, -1) < v2:
                            know[k2] = v2
            if cur_barrier is not None:
                for sem, cnt in cur_barrier[1]:
                    key = ("dma", sem.name)
                    if know.get(key, -1) < cnt:
                        if key not in best or best[key][0] < cnt:
                            best[key] = (cnt, None)
                        know[key] = cnt
            if o.extra_dma_waits:
                for sem, cnt in o.extra_dma_waits:
                    key = ("dma", sem.name)
                    if know.get(key, -1) < cnt:
                        if key not in best or best[key][0] < cnt:
                            best[key] = (cnt, None)
                        know[key] = cnt
            wait_edges[i] = [(k, v, w) for k, (v, w) in best.items()]
            op_knowledge[id(o)] = dict(know)

        for i, o in enumerate(self.ops):
            for key, val, w in wait_edges[i]:
                if key[0] == "eng":
                    w.inc_needed = True
        inc_counts = defaultdict(int)
        for o in self.ops:
            if o.dma_sem is None and o.inc_needed:
                inc_counts[o.engine] += 1
                o.inc_count = inc_counts[o.engine]
        for i, o in enumerate(self.ops):
            best = {}
            for key, val, w in wait_edges[i]:
                v = w.inc_count if (key[0] == "eng") else val
                if best.get(key, -1) < v:
                    best[key] = v
            o.waits = sorted(best.items(), key=lambda kv: (kv[0][0], str(kv[0][1])))
        self._eng_ops = eng_ops
        return self

    def emit(self, nc, es):
        eng_sems = {}
        for e in ENGINES:
            if any(o.inc_needed for o in self._eng_ops.get(e, ())):
                eng_sems[e] = es.enter_context(nc.semaphore(f"es_{e}"))
        for name, dssem in self.dma_sems.items():
            dssem.handle = es.enter_context(nc.semaphore(f"ds_{name}"))

        def sem_handle(key):
            return eng_sems[key[1]] if key[0] == "eng" else self.dma_sems[key[1]].handle

        block = es.enter_context(nc.Block())

        def make_body(e):
            def body(eng):
                for o in self._eng_ops.get(e, ()):
                    for key, v in o.waits:
                        eng.wait_ge(sem_handle(key), v)
                    ins = o.emit_fn(eng)
                    if o.dma_sem is not None:
                        ins.then_inc(o.dma_sem.handle, 16)
                    elif o.inc_needed:
                        ins.then_inc(eng_sems[o.engine], 1)
            return body

        for e in ENGINES:
            if self._eng_ops.get(e):
                getattr(block, e)(make_body(e))

    def stats(self):
        from collections import Counter
        c = Counter(o.engine for o in self.ops)
        return (f"ops={dict(c)} total={len(self.ops)} "
                f"waits={sum(len(o.waits) for o in self.ops)} "
                f"incs={sum(1 for o in self.ops if o.inc_needed)} "
                f"dma_sems={len(self.dma_sems)}")



def build_program():
    PH = os.environ.get("MK_PHASES", "ABCD")
    PREP = os.environ.get("MK_PREP", "all")
    ABITS = os.environ.get("MK_ABITS", "full")
    AJ = int(os.environ.get("MK_AJ", "16"))
    ABATCH = int(os.environ.get("MK_ABATCH", str(BL)))
    nc = bass.Bass("TRN2", target_bir_lowering=False)

    xtin = nc.dram_tensor("xtin", [BL, D, S], F32, kind="ExternalInput").ap()
    wqt = nc.dram_tensor("wqt", [D, D], F32, kind="ExternalInput").ap()
    wkt = nc.dram_tensor("wkt", [D, D], F32, kind="ExternalInput").ap()
    wvt = nc.dram_tensor("wvt", [D, D], F32, kind="ExternalInput").ap()
    wot = nc.dram_tensor("wot", [D, D], F32, kind="ExternalInput").ap()
    w1 = nc.dram_tensor("w1", [D, DFF], F32, kind="ExternalInput").ap()
    w2 = nc.dram_tensor("w2", [DFF, D], F32, kind="ExternalInput").ap()
    qb2 = nc.dram_tensor("qb2", [128, NCK], F32, kind="ExternalInput").ap()
    kb2 = nc.dram_tensor("kb2", [128, NCK], F32, kind="ExternalInput").ap()
    vbr = nc.dram_tensor("vbr", [128, D], F32, kind="ExternalInput").ap()
    ob2 = nc.dram_tensor("ob2", [128, NCK], F32, kind="ExternalInput").ap()
    b1c = nc.dram_tensor("b1c", [128, 64], F32, kind="ExternalInput").ap()
    b2c = nc.dram_tensor("b2c", [128, NCK], F32, kind="ExternalInput").ap()
    rwc = nc.dram_tensor("rwc", [128, NCK], F32, kind="ExternalInput").ap()
    rbx = nc.dram_tensor("rbx", [1, 1], F32, kind="ExternalInput").ap()
    cosk = nc.dram_tensor("cosk", [128, S], BF16, kind="ExternalInput").ap()
    sink = nc.dram_tensor("sink", [128, S], BF16, kind="ExternalInput").ap()
    idfh = nc.dram_tensor("idf", [128, 128], F32, kind="ExternalInput").ap()
    onesch = nc.dram_tensor("onesc", [128, 1], BF16, kind="ExternalInput").ap()
    onesrh = nc.dram_tensor("onesr", [1, 128], BF16, kind="ExternalInput").ap()
    ones1fh = nc.dram_tensor("ones1f", [1, 128], F32, kind="ExternalInput").ap()

    yt = nc.dram_tensor("yt", [BL, D, S], F32, kind="ExternalOutput").ap()

    wqt_b = nc.dram_tensor("wqt_b", [D, D], BF16, kind="Internal").ap()
    wkt_b = nc.dram_tensor("wkt_b", [D, D], BF16, kind="Internal").ap()
    wvt_b = nc.dram_tensor("wvt_b", [D, D], BF16, kind="Internal").ap()
    wot_b = nc.dram_tensor("wot_b", [D, D], BF16, kind="Internal").ap()
    w1_b = nc.dram_tensor("w1_b", [D, DFF], BF16, kind="Internal").ap()
    w2_b = nc.dram_tensor("w2_b", [DFF, D], BF16, kind="Internal").ap()
    qt_s = nc.dram_tensor("qt_s", [BL, H, 128, S], BF16, kind="Internal").ap()
    kt_s = nc.dram_tensor("kt_s", [BL, H, 128, S], BF16, kind="Internal").ap()
    v_s = nc.dram_tensor("v_s", [BL, H, 16, 128, 128], BF16, kind="Internal").ap()
    x1_s = nc.dram_tensor("x1_s", [BL, NCK, 128, S], F32, kind="Internal").ap()
    x1t_b = nc.dram_tensor("x1t_b", [NCK, 128, TOK], BF16, kind="Internal").ap()

    p = Prog()
    B = p.buf
    es = ExitStack()
    with es:
        idf = nc.alloc_sbuf_tensor("c_idf", [128, 128], F32)
        onesc = nc.alloc_sbuf_tensor("c_onesc", [128, 1], BF16)
        onesr = nc.alloc_sbuf_tensor("c_onesr", [1, 128], BF16)
        ones1f = nc.alloc_sbuf_tensor("c_ones1f", [1, 128], F32)
        s_qb = nc.alloc_sbuf_tensor("c_qb", [128, NCK], F32)
        s_kb = nc.alloc_sbuf_tensor("c_kb", [128, NCK], F32)
        s_ob = nc.alloc_sbuf_tensor("c_ob", [128, NCK], F32)
        s_b1 = nc.alloc_sbuf_tensor("c_b1", [128, 64], F32)
        s_b2 = nc.alloc_sbuf_tensor("c_b2", [128, NCK], F32)
        s_rwf = nc.alloc_sbuf_tensor("c_rwf", [128, NCK], F32)
        s_rw = nc.alloc_sbuf_tensor("c_rw", [128, NCK], BF16)
        s_rb = nc.alloc_sbuf_tensor("c_rb", [1, 1], F32)
        s_rbh = nc.alloc_sbuf_tensor("c_rbh", [1, 1], F32)
        s_qbs = nc.alloc_sbuf_tensor("c_qbs", [128, NCK], F32)
        vbrep = nc.alloc_sbuf_tensor("c_vbrep", [128, D], F32)
        negshift = nc.alloc_sbuf_tensor("c_negsh", [128, 2], F32)
        maxcol = nc.alloc_sbuf_tensor("c_maxcol", [128, 1], F32)
        negsh1 = nc.alloc_sbuf_tensor("c_negsh1", [1, 1], F32)
        lrepA = nc.alloc_sbuf_tensor("c_lrepA", [128, 512], F32)
        lrepB = nc.alloc_sbuf_tensor("c_lrepB", [128, 512], F32)
        cx2st = nc.alloc_sbuf_tensor("c_x2st", [128, 2, 512], F32)
        cx1tst = nc.alloc_sbuf_tensor("c_x1tst", [128, 2, 512], BF16)
        chcv = nc.alloc_sbuf_tensor("c_hcv", [128, 2, 2, 512], BF16)
        cxcf = nc.alloc_sbuf_tensor("c_xcf", [128, 2, 512], F32)
        rows = nc.alloc_sbuf_tensor("c_rows", [1, 7, 512], F32)
        rowsb = nc.alloc_sbuf_tensor("c_rowsb", [1, 6, 512], BF16)
        s_eps = nc.alloc_sbuf_tensor("c_eps", [1, 1], F32)

        for ap_h, t, nm in ((idfh, idf, "idf"), (onesch, onesc, "onesc"),
                            (onesrh, onesr, "onesr"), (ones1fh, ones1f, "ones1f"),
                            (qb2, s_qb, "qb"), (kb2, s_kb, "kb"),
                            (ob2, s_ob, "ob"), (b1c, s_b1, "b1"),
                            (b2c, s_b2, "b2"), (rbx, s_rb, "rb")):
            p.dma("sync", "cst", lambda e, a=ap_h, t=t: e.dma_start(out=t[:], in_=a),
                  writes=[B(nm)])
        p.dma("sync", "cst",
              lambda e: e.dma_start(out=s_rwf[:], in_=rwc), writes=[B("rwf")])
        p.op("vector", lambda e: e.tensor_copy(s_rw[:], s_rwf[:]),
             reads=[B("rwf")], writes=[B("rw")])
        p.op("scalar", lambda e: e.mul(s_rbh[:], s_rb[:], 0.5),
             reads=[B("rb")], writes=[B("rbh")])
        p.op("scalar", lambda e: e.mul(s_qbs[:], s_qb[:], DH ** -0.5),
             reads=[B("qb")], writes=[B("qbs")])
        p.op("vector", lambda e: e.memset(negshift[:], 0.0),
             writes=[B("sh0"), B("sh1")])
        p.op("vector", lambda e: e.memset(s_eps[:], 1e-5), writes=[B("eps")])

        def cast_w(src, dst, rows_total, cols, nm, chunk=128):
            for r0 in range(0, rows_total, chunk):
                p.dma("gpsimd", f"prep_{nm}",
                      lambda e, s=src[r0:r0 + chunk, :], d=dst[r0:r0 + chunk, :]:
                      e.dma_start(out=d, in_=s),
                      writes=[B(nm)], tag=f"cast_{nm}")
        if PREP in ("all", "casts"):
            cast_w(wqt, wqt_b, D, D, "wqt_b")
            cast_w(wkt, wkt_b, D, D, "wkt_b")
            cast_w(wvt, wvt_b, D, D, "wvt_b")
            cast_w(wot, wot_b, D, D, "wot_b")
            cast_w(w1, w1_b, D, DFF, "w1_b")
            cast_w(w2, w2_b, DFF, D, "w2_b")

        p.dma("sync", "cst", lambda e: e.dma_start(out=vbrep[:], in_=vbr),
              writes=[B("vbrep")])
        if PREP == "bar":
            p.barrier("pvb0")

        def rep_row(slot, sb, ps, dest, dest_buf, pref):
            hi_b = B(f"{pref}_hi")
            lo_b = B(f"{pref}_lo")
            p.op("scalar",
                 lambda e: e.copy(rowsb[:, sb, :], rows[:, slot, :]),
                 reads=[B(f"{pref}_row")], writes=[hi_b])
            p.op("vector",
                 lambda e: e.tensor_sub(rowsb[:, sb + 1, :], rows[:, slot, :],
                                        rowsb[:, sb, :]),
                 reads=[B(f"{pref}_row"), hi_b], writes=[lo_b])
            p.op("tensor",
                 lambda e: e.matmul(ps, onesr[:], rowsb[:, sb, :],
                                    start=True, stop=False),
                 reads=[B("onesr"), hi_b], writes=[B(f"{pref}_ps")])
            p.op("tensor",
                 lambda e: e.matmul(ps, onesr[:], rowsb[:, sb + 1, :],
                                    start=False, stop=True),
                 reads=[B("onesr"), lo_b], writes=[B(f"{pref}_ps")])
            p.op("vector",
                 lambda e: e.tensor_copy(dest, ps),
                 reads=[B(f"{pref}_ps")], writes=[dest_buf])

        for b in range(min(BL, ABATCH) if "A" in PH else 0):
            with ExitStack() as ea:
                XT = ea.enter_context(nc.sbuf_tensor(f"a_xt{b}", [128, NCK, S], BF16))
                t_cosk = ea.enter_context(nc.sbuf_tensor(f"a_cosk{b}", [128, S], BF16))
                t_sink = ea.enter_context(nc.sbuf_tensor(f"a_sink{b}", [128, S], BF16))
                for ap_h, t, nm in ((cosk, t_cosk, "cosk"), (sink, t_sink, "sink")):
                    p.dma("sync", f"cstA{b}",
                          lambda e, a=ap_h, t=t: e.dma_start(out=t[:], in_=a),
                          writes=[B(nm)])
                xtf = ea.enter_context(nc.sbuf_tensor(f"a_xtf{b}", [128, 2, 1024], F32))
                wqk = ea.enter_context(nc.sbuf_tensor(f"a_wqk{b}", [128, 2, 2, NCK, 128], BF16))
                wvb = ea.enter_context(nc.sbuf_tensor(f"a_wv{b}", [128, 2, NCK, 512], BF16))
                abuf = ea.enter_context(nc.sbuf_tensor(f"a_ab{b}", [128, 3, 512], BF16))
                asw = ea.enter_context(nc.sbuf_tensor(f"a_as{b}", [128, 3, 512], BF16))
                tt1 = ea.enter_context(nc.sbuf_tensor(f"a_t1{b}", [128, 2, 512], BF16))
                tt2 = ea.enter_context(nc.sbuf_tensor(f"a_t2{b}", [128, 2, 512], BF16))
                qst = ea.enter_context(nc.sbuf_tensor(f"a_qst{b}", [128, 2, S], BF16))
                vst = ea.enter_context(nc.sbuf_tensor(f"a_vst{b}", [128, 2, 512], BF16))
                psA = [ea.enter_context(nc.psum_tensor(f"a_psA{b}{r}", [128, 512], F32))
                       for r in range(2)]
                psV = [ea.enter_context(nc.psum_tensor(f"a_psV{b}{r}", [128, 512], F32))
                       for r in range(2)]

                for ch in range(NCK * 2):
                    c, hf = ch // 2, ch % 2
                    p.dma("sync", f"a_x{ch % 2}",
                          lambda e, c=c, hf=hf, ch=ch, b=b: e.dma_start(
                              out=xtf[:, ch % 2, :],
                              in_=xtin[b, c * 128:(c + 1) * 128,
                                       hf * 1024:(hf + 1) * 1024]),
                          reads=[B(f"xtf{ch % 2}")], writes=[B(f"xtfw{ch % 2}")])
                    p.op("scalar",
                         lambda e, c=c, hf=hf, ch=ch: e.copy(
                             XT[:, c, hf * 1024:(hf + 1) * 1024], xtf[:, ch % 2, :]),
                         reads=[B(f"xtfw{ch % 2}")],
                         writes=[B("XT"), B(f"xtf{ch % 2}")], tag="xtcv")

                def wq_load(w, j):
                    src = (wqt_b if w == 0 else wkt_b)
                    nm = ("wqt_b" if w == 0 else "wkt_b")
                    rr = src[:, j * 128:(j + 1) * 128].rearrange(
                        "(c p) n -> p c n", p=128)
                    p.dma("sync", f"a_w{w}{j % 2}",
                          lambda e, rr=rr, w=w, j=j: e.dma_start(
                              out=wqk[:, w, j % 2, :, :], in_=rr),
                          reads=[B(nm), B(f"wqk{w}{j % 2}")],
                          writes=[B(f"wqkw{w}{j % 2}")])

                def qk_block(w, j):
                    tabc = t_cosk
                    tabs = t_sink
                    bias = s_qbs if w == 0 else s_kb
                    scl = DH ** -0.5 if w == 0 else 1.0
                    wb = B(f"wqkw{w}{j % 2}")
                    for c in range(4):
                        sr = (j * 4 + c) % 2
                        ar = (j * 4 + c) % 3
                        sl = slice(c * 512, (c + 1) * 512)
                        for ck in range(NCK):
                            p.op("tensor",
                                 lambda e, w=w, j=j, ck=ck, sl=sl, sr=sr: e.matmul(
                                     psA[sr][:], wqk[:, w, j % 2, ck, :],
                                     XT[:, ck, sl],
                                     start=(ck == 0), stop=(ck == NCK - 1)),
                                 reads=[wb, B("XT")],
                                 writes=[B(f"psA{sr}")], tag="qkmm")
                        if ABITS == "xqm":
                            continue
                        if ABITS == "xqc":
                            p.op("scalar",
                                 lambda e, sr=sr, ar=ar: e.copy(
                                     abuf[:, ar, :], psA[sr][:]),
                                 reads=[B(f"psA{sr}")],
                                 writes=[B(f"ab{ar}"), B(f"psA{sr}")], tag="qkev")
                        else:
                            p.op("scalar",
                                 lambda e, j=j, sr=sr, ar=ar, bias=bias, scl=scl: e.activation(
                                     abuf[:, ar, :], psA[sr][:], AF.Identity,
                                     bias=bias[:, j:j + 1], scale=scl),
                                 reads=[B(f"psA{sr}"), B("qbs" if w == 0 else "kb")],
                                 writes=[B(f"ab{ar}"), B(f"psA{sr}")], tag="qkev")
                        if ABITS in ("xq", "xqm", "xqc"):
                            continue
                        p.dma("scalar", f"a_sw{ar}",
                              lambda e, ar=ar: e.dma_start(out=asw[0:64, ar, :],
                                                           in_=abuf[64:128, ar, :]),
                              reads=[B(f"ab{ar}")], writes=[B(f"as{ar}")])
                        p.dma("scalar", f"a_sw{ar}",
                              lambda e, ar=ar: e.dma_start(out=asw[64:128, ar, :],
                                                           in_=abuf[0:64, ar, :]),
                              reads=[B(f"ab{ar}")], writes=[B(f"as{ar}")])
                        tr = (j * 4 + c) % 2
                        tcn = "cosk"
                        tsn = "sink"
                        p.op("vector",
                             lambda e, ar=ar, tr=tr, sl=sl, tabc=tabc: e.tensor_mul(
                                 tt1[:, tr, :], abuf[:, ar, :], tabc[:, sl]),
                             reads=[B(f"ab{ar}"), B(tcn)],
                             writes=[B(f"tt1{tr}")], tag="rope1")
                        p.op("vector",
                             lambda e, ar=ar, tr=tr, sl=sl, tabs=tabs: e.tensor_mul(
                                 tt2[:, tr, :], asw[:, ar, :], tabs[:, sl]),
                             reads=[B(f"as{ar}"), B(tsn)],
                             writes=[B(f"tt2{tr}"), B(f"ab{ar}")], tag="rope2")
                        p.op("vector",
                             lambda e, tr=tr, sl=sl, j=j: e.tensor_add(
                                 qst[:, j % 2, sl], tt1[:, tr, :], tt2[:, tr, :]),
                             reads=[B(f"tt1{tr}"), B(f"tt2{tr}")],
                             writes=[B(f"qst{j % 2}"), B(f"as{(j * 4 + c) % 3}")],
                             tag="rope3")
                    if ABITS in ("xq", "xqm", "xqc", "xqr"):
                        return
                    dst = (qt_s if w == 0 else kt_s)[b, j]
                    p.dma("scalar", f"a_sp{w}{j % 2}",
                          lambda e, dst=dst, j=j: e.dma_start(out=dst,
                                                              in_=qst[:, j % 2, :]),
                          reads=[B(f"qst{j % 2}")], writes=[B(f"qkspill{w}_{j}")])

                wq_load(0, 0)
                wq_load(1, 0)
                for j in range(min(H, AJ) if ABITS != "x" else 0):
                    if j + 1 < min(H, AJ):
                        wq_load(0, j + 1)
                        wq_load(1, j + 1)
                    qk_block(0, j)
                    qk_block(1, j)

                def wv_load(c4):
                    rr = wvt_b[:, c4 * 512:(c4 + 1) * 512].rearrange(
                        "(c p) n -> p c n", p=128)
                    p.dma("sync", f"a_wv{c4 % 2}",
                          lambda e, rr=rr, c4=c4: e.dma_start(
                              out=wvb[:, c4 % 2, :, :], in_=rr),
                          reads=[B("wvt_b"), B(f"wvb{c4 % 2}")],
                          writes=[B(f"wvbw{c4 % 2}")])

                wv_load(0)
                for c4 in range(4 if ABITS == "full" else 0):
                    if c4 + 1 < 4:
                        wv_load(c4 + 1)
                    dsl = slice(c4 * 512, (c4 + 1) * 512)
                    for t in range(16):
                        vr = (c4 * 16 + t) % 2
                        for ck in range(NCK):
                            p.op("tensor",
                                 lambda e, ck=ck, t=t, c4=c4, vr=vr: e.matmul(
                                     psV[vr][:], XT[:, ck, t * 128:(t + 1) * 128],
                                     wvb[:, c4 % 2, ck, :],
                                     start=(ck == 0), stop=(ck == NCK - 1)),
                                 reads=[B(f"wvbw{c4 % 2}"), B("XT")],
                                 writes=[B(f"psV{vr}")], tag="vmm")
                        p.op("vector",
                             lambda e, vr=vr, dsl=dsl: e.scalar_tensor_tensor(
                                 vst[:, vr, :], psV[vr][:], 1.0, vbrep[:, dsl],
                                 ALU.mult, ALU.add),
                             reads=[B(f"psV{vr}"), B("vbrep")],
                             writes=[B(f"vst{vr}"), B(f"psV{vr}")], tag="vev")
                        for hh in range(4):
                            h = c4 * 4 + hh
                            p.dma("scalar", f"a_spv{vr}",
                                  lambda e, h=h, t=t, vr=vr, hh=hh, b=b: e.dma_start(
                                      out=v_s[b, h, t],
                                      in_=vst[:, vr, hh * 128:(hh + 1) * 128]),
                                  reads=[B(f"vst{vr}")], writes=[B(f"vspill{h}_{t}")])
                p.barrier(f"ab{b}")

            if "B" not in PH:
                continue
            with ExitStack() as eo:
                OT = eo.enter_context(nc.sbuf_tensor(f"b_ot{b}", [128, NCK, S], BF16))

                with ExitStack() as eb:
                    qtb = eb.enter_context(nc.sbuf_tensor(f"b_qt{b}", [128, 2, S], BF16))
                    ktb = eb.enter_context(nc.sbuf_tensor(f"b_kt{b}", [128, 2, S], BF16))
                    vtb = eb.enter_context(nc.sbuf_tensor(f"b_vt{b}", [128, 2, 16, 128], BF16))
                    pt = eb.enter_context(nc.sbuf_tensor(f"b_pt{b}", [128, 5, 1024], BF16))
                    lrec = eb.enter_context(nc.sbuf_tensor(f"b_rc{b}", [1, 2, 512], F32))
                    lrhi = eb.enter_context(nc.sbuf_tensor(f"b_rh{b}", [1, 2, 512], BF16))
                    lrlo = eb.enter_context(nc.sbuf_tensor(f"b_rl{b}", [1, 2, 512], BF16))
                    reps = cxcf
                    psS = [eb.enter_context(
                        nc.psum_tensor(f"b_psS{b}{r}", [128, 1024], F32))
                        for r in range(2)]
                    psO = [eb.enter_context(
                        nc.psum_tensor(f"b_psO{b}{r}", [128, 512], F32))
                        for r in range(2)]
                    psR = [eb.enter_context(
                        nc.psum_tensor(f"b_psR{b}{r}", [128, 512], F32))
                        for r in range(2)]
                    psD = [psR[r][0:1, :] for r in range(2)]
                    psT = psS[0][0:1, 0:128]
                    psB_ = psS[0][0:128, 512 + 256:512 + 257]

                    def head_load(h):
                        hr = h % 2
                        p.dma("sync", f"b_hq{hr}",
                              lambda e, h=h, hr=hr, b=b: e.dma_start(
                                  out=qtb[:, hr, :], in_=qt_s[b, h]),
                              reads=[B(f"qkspill0_{h}"), B(f"qtb{hr}")],
                              writes=[B(f"qtbw{hr}")])
                        p.dma("sync", f"b_hk{hr}",
                              lambda e, h=h, hr=hr, b=b: e.dma_start(
                                  out=ktb[:, hr, :], in_=kt_s[b, h]),
                              reads=[B(f"qkspill1_{h}"), B(f"ktb{hr}")],
                              writes=[B(f"ktbw{hr}")])
                        rr = v_s[b, h].rearrange("t p d -> p t d")
                        p.dma("sync", f"b_hv{hr}",
                              lambda e, rr=rr, hr=hr: e.dma_start(
                                  out=vtb[:, hr, :, :], in_=rr),
                              reads=[B(f"vtb{hr}")] + [B(f"vspill{h}_{t}") for t in range(16)],
                              writes=[B(f"vtbw{hr}")])

                    def b_scores(h, c, g):
                        hr = h % 2
                        sr = g % 2
                        for hf in range(2):
                            j = 2 * g + hf
                            p.op("tensor",
                                 lambda e, j=j, sr=sr, hf=hf, c=c, hr=hr: e.matmul(
                                     psS[sr][:, hf * 512:(hf + 1) * 512],
                                     ktb[:, hr, j * 128:(j + 1) * 128],
                                     qtb[:, hr, c * 512:(c + 1) * 512],
                                     start=True, stop=True),
                                 reads=[B(f"ktbw{hr}"), B(f"qtbw{hr}")],
                                 writes=[B(f"psS{sr}")], tag="sc")

                    def b_shift_chain(h):
                        w = (h + 1) % 2
                        p.op("vector",
                             lambda e: e.reduce_max(maxcol[:], psS[0][:, 0:512],
                                                    AX.X, negate=True),
                             reads=[B("psS0")], writes=[B("maxcol")])
                        p.op("tensor", lambda e: e.transpose(psT, maxcol[:], idf[:]),
                             reads=[B("maxcol"), B("idf")], writes=[B("psS0")])
                        p.op("vector",
                             lambda e: e.tensor_reduce(negsh1[:], psT, AX.X, ALU.min),
                             reads=[B("psS0")], writes=[B("negsh1")])
                        p.op("tensor",
                             lambda e: e.matmul(psB_, ones1f[:], negsh1[:],
                                                start=True, stop=True),
                             reads=[B("negsh1"), B("ones1f")], writes=[B("psS0")])
                        p.op("scalar",
                             lambda e, w=w: e.copy(negshift[:, w:w + 1], psB_),
                             reads=[B("psS0")], writes=[B(f"sh{w}")])

                    def b_pv_dsum(h, c, g):
                        hr = h % 2
                        oc = c % 2
                        pr = g % 5
                        for hf in range(2):
                            j = 2 * g + hf
                            p.op("tensor",
                                 lambda e, j=j, oc=oc, pr=pr, hf=hf, hr=hr: e.matmul(
                                     psO[oc][:], vtb[:, hr, j, :],
                                     pt[:, pr, hf * 512:(hf + 1) * 512],
                                     start=(j == 0), stop=(j == 15)),
                                 reads=[B(f"vtbw{hr}"), B(f"pt{pr}")],
                                 writes=[B(f"psO{oc}")], tag="pv")
                            p.op("tensor",
                                 lambda e, j=j, m=c % 2, pr=pr, hf=hf: e.matmul(
                                     psD[m], onesc[:],
                                     pt[:, pr, hf * 512:(hf + 1) * 512],
                                     start=(j == 0), stop=(j == 15)),
                                 reads=[B("onesc"), B(f"pt{pr}")],
                                 writes=[B(f"bkDR{c % 2}")], tag="ds")

                    def b_epi_scalar(c):
                        m = c % 2
                        p.op("vector",
                             lambda e, m=m: e.reciprocal(lrec[:, m, :], psD[m]),
                             reads=[B(f"bkDR{m}")], writes=[B(f"lrec{m}")])
                        p.op("scalar",
                             lambda e, m=m: e.copy(lrhi[:, m, :], lrec[:, m, :]),
                             reads=[B(f"lrec{m}")], writes=[B(f"lrhi{m}")])
                        p.op("vector",
                             lambda e, m=m: e.tensor_sub(lrlo[:, m, :],
                                                         lrec[:, m, :],
                                                         lrhi[:, m, :]),
                             reads=[B(f"lrec{m}"), B(f"lrhi{m}")],
                             writes=[B(f"lrlo{m}")])

                    def b_epi_pe(h, c):
                        m = c % 2
                        p.op("tensor",
                             lambda e, m=m: e.matmul(psR[m][:], onesr[:],
                                                     lrhi[:, m, :],
                                                     start=True, stop=False),
                             reads=[B("onesr"), B(f"lrhi{m}")],
                             writes=[B(f"bkDR{m}")])
                        p.op("tensor",
                             lambda e, m=m: e.matmul(psR[m][:], onesr[:],
                                                     lrlo[:, m, :],
                                                     start=False, stop=True),
                             reads=[B("onesr"), B(f"lrlo{m}")],
                             writes=[B(f"bkDR{m}")])
                        p.op("vector",
                             lambda e, m=m: e.tensor_copy(reps[:, m, :], psR[m][:]),
                             reads=[B(f"bkDR{m}")], writes=[B(f"reps{m}")])
                        p.op("vector",
                             lambda e, h=h, c=c, m=m: e.tensor_mul(
                                 OT[:, h, c * 512:(c + 1) * 512], psO[m][:],
                                 reps[:, m, :]),
                             reads=[B(f"psO{m}"), B(f"reps{m}")],
                             writes=[B("OT")], tag="oev")

                    LAG = 4
                    head_load(0)
                    pend = [None]
                    for h in range(H):
                        if h + 1 < H:
                            head_load(h + 1)
                        rd = h % 2
                        for c in range(4):
                            b_scores(h, c, 0)
                            for g in range(8):
                                sr = g % 2
                                pr = g % 5
                                if g + 1 < 8:
                                    b_scores(h, c, g + 1)
                                p.op("scalar",
                                     lambda e, sr=sr, pr=pr, rd=rd: e.activation(
                                         pt[:, pr, :], psS[sr][:], AF.Exp,
                                         bias=negshift[:, rd:rd + 1], scale=1.0),
                                     reads=[B(f"psS{sr}"), B(f"sh{rd}")],
                                     writes=[B(f"pt{pr}"), B(f"psS{sr}")], tag="exp")
                                if g == 0 and c == 3:
                                    b_shift_chain(h)
                                if g >= LAG:
                                    b_pv_dsum(h, c, g - LAG)
                                if g == 5 and pend[0] is not None:
                                    b_epi_pe(*pend[0])
                                    pend[0] = None
                            for g in range(8 - LAG, 8):
                                b_pv_dsum(h, c, g)
                            b_epi_scalar(c)
                            pend[0] = (h, c)
                    b_epi_pe(*pend[0])
                    pend[0] = None
                    p.barrier(f"bb{b}")

                if "C" not in PH:
                    continue
                with ExitStack() as ec:
                    wob = ec.enter_context(nc.sbuf_tensor(f"c_wo{b}", [128, 2, NCK, 128], BF16))
                    hbuf = ec.enter_context(nc.sbuf_tensor(f"c_h{b}", [128, NCK, 512], F32))
                    xcf, hcv, x2st, x1tst = cxcf, chcv, cx2st, cx1tst
                    psY = [ec.enter_context(
                        nc.psum_tensor(f"c_psY{b}{r}", [128, 512], F32))
                        for r in range(2)]
                    psSt = [ec.enter_context(
                        nc.psum_tensor(f"c_psSt{b}{r}", [128, 512], F32))
                        for r in range(2)]
                    psRp = [ec.enter_context(
                        nc.psum_tensor(f"c_psRp{b}{r}", [128, 512], F32))
                        for r in range(2)]

                    def wo_load(c, j):
                        rr = wot_b[:, j * 128:(j + 1) * 128].rearrange(
                            "(c p) n -> p c n", p=128)
                        p.dma("sync", f"c_w{j % 2}",
                              lambda e, rr=rr, j=j: e.dma_start(
                                  out=wob[:, j % 2, :, :], in_=rr),
                              reads=[B("wot_b"), B(f"wob{j % 2}")],
                              writes=[B(f"wobw{j % 2}")])

                    def xc_load(c, j):
                        p.dma("sync", f"c_x{j % 2}",
                              lambda e, c=c, j=j, b=b: e.dma_start(
                                  out=xcf[:, j % 2, :],
                                  in_=xtin[b, j * 128:(j + 1) * 128,
                                           c * 512:(c + 1) * 512]),
                              reads=[B(f"xcf{j % 2}")], writes=[B(f"xcfw{j % 2}")])

                    for c in range(4):
                        csl = slice(c * 512, (c + 1) * 512)
                        wo_load(c, 0)
                        xc_load(c, 0)
                        for j in range(NCK):
                            yr = j % 2
                            if j + 1 < NCK:
                                wo_load(c, j + 1)
                                xc_load(c, j + 1)
                            for ck in range(NCK):
                                p.op("tensor",
                                     lambda e, ck=ck, j=j, csl=csl, yr=yr: e.matmul(
                                         psY[yr][:], wob[:, j % 2, ck, :],
                                         OT[:, ck, csl],
                                         start=(ck == 0), stop=(ck == NCK - 1)),
                                     reads=[B(f"wobw{j % 2}"), B("OT")],
                                     writes=[B(f"psY{yr}")], tag="yomm")
                            p.op("vector",
                                 lambda e, j=j, yr=yr: e.scalar_tensor_tensor(
                                     hbuf[:, j, :], psY[yr][:], s_ob[:, j:j + 1],
                                     xcf[:, j % 2, :], ALU.add, ALU.add),
                                 reads=[B(f"psY{yr}"), B("ob"), B(f"xcfw{j % 2}")],
                                 writes=[B("hbuf"), B(f"psY{yr}"), B(f"xcf{j % 2}")],
                                 tag="resid")
                            p.op("scalar",
                                 lambda e, j=j: e.copy(hcv[:, j % 2, 0, :],
                                                       hbuf[:, j, :]),
                                 reads=[B("hbuf")], writes=[B(f"hcv{j % 2}")],
                                 tag="hcv")
                            p.op("scalar",
                                 lambda e, j=j: e.activation(hcv[:, j % 2, 1, :],
                                                             hbuf[:, j, :],
                                                             AF.Square),
                                 reads=[B("hbuf")], writes=[B(f"hcq{j % 2}")],
                                 tag="hsq")
                            p.op("tensor",
                                 lambda e, j=j: e.matmul(
                                     psSt[0][0:1, :], onesc[:], hcv[:, j % 2, 0, :],
                                     start=(j == 0), stop=(j == NCK - 1)),
                                 reads=[B("onesc"), B(f"hcv{j % 2}")],
                                 writes=[B("psSt0")], tag="stm")
                            p.op("tensor",
                                 lambda e, j=j: e.matmul(
                                     psSt[1][0:1, :], onesc[:], hcv[:, j % 2, 1, :],
                                     start=(j == 0), stop=(j == NCK - 1)),
                                 reads=[B("onesc"), B(f"hcq{j % 2}")],
                                 writes=[B("psSt1")], tag="stm")
                        p.op("vector",
                             lambda e: e.tensor_scalar_mul(rows[:, 0, :],
                                                           psSt[0][0:1, :],
                                                           1.0 / D),
                             reads=[B("psSt0")], writes=[B("ln_mu")], tag="mu")
                        p.op("vector",
                             lambda e: e.tensor_scalar_mul(rows[:, 1, :],
                                                           psSt[1][0:1, :],
                                                           1.0 / D),
                             reads=[B("psSt1")], writes=[B("ln_msq"), B("psSt1")])
                        p.op("vector",
                             lambda e: e.scalar_tensor_tensor(
                                 rows[:, 2, :], rows[:, 0, :], 1.0, rows[:, 0, :],
                                 ALU.mult, ALU.mult),
                             reads=[B("ln_mu")], writes=[B("ln_mu2")])
                        p.op("vector",
                             lambda e: e.tensor_sub(rows[:, 1, :], rows[:, 1, :],
                                                    rows[:, 2, :]),
                             reads=[B("ln_msq"), B("ln_mu2")], writes=[B("ln_var")])
                        p.op("scalar",
                             lambda e: e.activation(rows[:, 2, :], rows[:, 1, :],
                                                    AF.Sqrt, bias=s_eps[:]),
                             reads=[B("ln_var"), B("eps")], writes=[B("ln_std")])
                        p.op("vector",
                             lambda e: e.reciprocal(rows[:, 3, :], rows[:, 2, :]),
                             reads=[B("ln_std")], writes=[B("ln_rstd")])
                        p.op("vector",
                             lambda e: e.tensor_scalar_mul(rows[:, 3, :],
                                                           rows[:, 3, :], 2.0),
                             reads=[B("ln_rstd")], writes=[B("ln_r2_row")])
                        p.op("vector",
                             lambda e: e.scalar_tensor_tensor(
                                 rows[:, 4, :], rows[:, 0, :], -1.0, rows[:, 3, :],
                                 ALU.mult, ALU.mult),
                             reads=[B("ln_mu"), B("ln_r2_row")],
                             writes=[B("ln_n2_row")])
                        rep_row(3, 0, psRp[0][:], lrepA[:], B("lnrepA"), "ln_r2")
                        rep_row(4, 2, psRp[1][:], lrepB[:], B("lnrepB"), "ln_n2")
                        for j in range(NCK):
                            jr = j % 2
                            p.op("vector",
                                 lambda e, j=j, jr=jr: e.tensor_mul(
                                     x2st[:, jr, :], hbuf[:, j, :], lrepA[:]),
                                 reads=[B("hbuf"), B("lnrepA"), B(f"x2st{jr}")],
                                 writes=[B(f"x2w{jr}")], tag="nrm1")
                            p.op("vector",
                                 lambda e, jr=jr: e.tensor_add(
                                     x2st[:, jr, :], x2st[:, jr, :], lrepB[:]),
                                 reads=[B(f"x2w{jr}"), B("lnrepB")],
                                 writes=[B(f"x2w{jr}")], tag="nrm2")
                            p.op("scalar",
                                 lambda e, jr=jr: e.activation(
                                     x1tst[:, jr, :], x2st[:, jr, :], AF.Copy,
                                     scale=0.5),
                                 reads=[B(f"x2w{jr}")], writes=[B(f"x1tw{jr}")],
                                 tag="x1cv")
                            p.dma("scalar", f"c_spx{jr}",
                                  lambda e, j=j, jr=jr, csl=csl, b=b: e.dma_start(
                                      out=x1_s[b, j][:, csl], in_=x2st[:, jr, :]),
                                  reads=[B(f"x2w{jr}")],
                                  writes=[B(f"x2st{jr}"), B(f"x1spill_{b}_{j}_{c}")])
                            p.dma("scalar", f"c_spt{jr}",
                                  lambda e, j=j, jr=jr, c=c, b=b: e.dma_start(
                                      out=x1t_b[j][:, b * S + c * 512:
                                                   b * S + (c + 1) * 512],
                                      in_=x1tst[:, jr, :]),
                                  reads=[B(f"x1tw{jr}")],
                                  writes=[B(f"x1tspill_{b}_{j}_{c}")])
                    p.barrier(f"cb{b}")

        with ExitStack() as ed:
            x1c = ed.enter_context(nc.sbuf_tensor("d_x1c", [128, NCK, 512], BF16))
            h1 = ed.enter_context(nc.sbuf_tensor("d_h1", [128, 64, 512], BF16))
            w1blk = ed.enter_context(nc.sbuf_tensor("d_w1", [128, 2, NCK, 128], BF16))
            w2blk = ed.enter_context(nc.sbuf_tensor("d_w2", [128, 2, 2, 32, 128], BF16))
            fbuf = ed.enter_context(nc.sbuf_tensor("d_f", [128, NCK, 512], F32))
            ftmp = ed.enter_context(nc.sbuf_tensor("d_ft", [128, 2, 512], F32))
            fcv = chcv
            x2ld = cxcf
            wrep = ed.enter_context(nc.sbuf_tensor("d_wrep", [128, 512], F32))
            drepA = lrepA
            drepB = lrepB
            outst = cx2st
            psH = [ed.enter_context(nc.psum_tensor(f"d_psH{r}", [128, 512], F32))
                   for r in range(2)]
            psY2 = [ed.enter_context(nc.psum_tensor(f"d_psY{r}", [128, 512], F32))
                    for r in range(2)]
            psSt2 = [ed.enter_context(nc.psum_tensor(f"d_psS{r}", [128, 512], F32))
                     for r in range(2)]
            psRp2 = [ed.enter_context(nc.psum_tensor(f"d_psR{r}", [128, 512], F32))
                     for r in range(2)]

            NQ = (TOK // 512) if "D" in PH else 0

            def d_loads(q):
                for ck in range(NCK):
                    p.dma("sync", "d_x1",
                          lambda e, ck=ck, q=q: e.dma_start(
                              out=x1c[:, ck, :],
                              in_=x1t_b[ck][:, q * 512:(q + 1) * 512]),
                          reads=[B(f"x1tspill_{q // 4}_{ck}_{q % 4}"), B("x1c")],
                          writes=[B("x1cw")], tag="dx1")

            def w1_load(q, jf):
                rr = w1_b[:, jf * 128:(jf + 1) * 128].rearrange(
                    "(c p) n -> p c n", p=128)
                p.dma("sync", f"d_w1{jf % 2}",
                      lambda e, rr=rr, jf=jf: e.dma_start(
                          out=w1blk[:, jf % 2, :, :], in_=rr),
                      reads=[B("w1_b"), B(f"w1blk{jf % 2}")],
                      writes=[B(f"w1w{jf % 2}")])

            def w2_load(q, j2):
                for hlf in range(2):
                    rr = w2_b[hlf * 4096:(hlf + 1) * 4096,
                              j2 * 128:(j2 + 1) * 128].rearrange(
                        "(c p) n -> p c n", p=128)
                    p.dma("sync", f"d_w2{j2 % 2}",
                          lambda e, rr=rr, j2=j2, hlf=hlf: e.dma_start(
                              out=w2blk[:, j2 % 2, hlf, :, :], in_=rr),
                          reads=[B("w2_b"), B(f"w2blk{j2 % 2}")],
                          writes=[B(f"w2w{j2 % 2}")])

            def x2_load(q, j2):
                bq, cq = q // 4, q % 4
                p.dma("sync", f"d_x2{j2 % 2}",
                      lambda e, j2=j2, bq=bq, cq=cq: e.dma_start(
                          out=x2ld[:, j2 % 2, :],
                          in_=x1_s[bq, j2][:, cq * 512:(cq + 1) * 512]),
                      reads=[B(f"x1spill_{bq}_{j2}_{cq}"), B(f"x2ld{j2 % 2}")],
                      writes=[B(f"x2w{j2 % 2}")])

            for q in range(NQ):
                d_loads(q)
                for ck in range(NCK):
                    p.op("tensor",
                         lambda e, ck=ck: e.matmul(
                             psSt2[0][0:1, :], s_rw[:, ck:ck + 1], x1c[:, ck, :],
                             start=(ck == 0), stop=(ck == NCK - 1)),
                         reads=[B("rw"), B("x1cw")], writes=[B("psSt20")],
                         tag="rt")
                p.op("scalar",
                     lambda e: e.activation(rows[:, 5, :], psSt2[0][0:1, :],
                                            AF.Tanh, bias=s_rbh[:], scale=0.5),
                     reads=[B("psSt20"), B("rbh")], writes=[B("rt_t")])
                p.op("vector",
                     lambda e: e.tensor_scalar(rows[:, 6, :], rows[:, 5, :],
                                               0.5, 0.5, ALU.mult, ALU.add),
                     reads=[B("rt_t")], writes=[B("rt_w_row")])
                rep_row(6, 4, psRp2[0][:], wrep[:], B("wrep"), "rt_w")

                w1_load(q, 0)
                for jf in range(64):
                    hr = jf % 2
                    if jf + 1 < 64:
                        w1_load(q, jf + 1)
                    for ck in range(NCK):
                        p.op("tensor",
                             lambda e, ck=ck, jf=jf, hr=hr: e.matmul(
                                 psH[hr][:], w1blk[:, jf % 2, ck, :],
                                 x1c[:, ck, :],
                                 start=(ck == 0), stop=(ck == NCK - 1)),
                             reads=[B(f"w1w{jf % 2}"), B("x1cw")],
                             writes=[B(f"psH{hr}")], tag="f1mm")
                    p.op("scalar",
                         lambda e, jf=jf, hr=hr: e.activation(
                             h1[:, jf, :], psH[hr][:], AF.Gelu_apprx_tanh,
                             bias=s_b1[:, jf:jf + 1], scale=1.0),
                         reads=[B(f"psH{hr}"), B("b1")],
                         writes=[B(f"h1_{jf}"), B(f"psH{hr}")], tag="gelu")

                w2_load(q, 0)
                x2_load(q, 0)
                for j2 in range(NCK):
                    yr = j2 % 2
                    if j2 + 1 < NCK:
                        w2_load(q, j2 + 1)
                        x2_load(q, j2 + 1)
                    for jf in range(64):
                        p.op("tensor",
                             lambda e, jf=jf, j2=j2, yr=yr: e.matmul(
                                 psY2[yr][:],
                                 w2blk[:, j2 % 2, jf // 32, jf % 32, :],
                                 h1[:, jf, :],
                                 start=(jf == 0), stop=(jf == 63)),
                             reads=[B(f"w2w{j2 % 2}"), B(f"h1_{jf}")],
                             writes=[B(f"psY2{yr}")], tag="f2mm")
                    p.op("vector",
                         lambda e, j2=j2, yr=yr: e.scalar_tensor_tensor(
                             ftmp[:, yr, :], psY2[yr][:], s_b2[:, j2:j2 + 1],
                             wrep[:], ALU.add, ALU.mult),
                         reads=[B(f"psY2{yr}"), B("b2"), B("wrep")],
                         writes=[B(f"ftmp{yr}"), B(f"psY2{yr}")], tag="fep1")
                    p.op("vector",
                         lambda e, j2=j2, yr=yr: e.tensor_add(
                             fbuf[:, j2, :], ftmp[:, yr, :], x2ld[:, yr, :]),
                         reads=[B(f"ftmp{yr}"), B(f"x2w{yr}")],
                         writes=[B("fbuf"), B(f"x2ld{yr}")], tag="fep2")
                    p.op("scalar",
                         lambda e, j2=j2: e.copy(fcv[:, j2 % 2, 0, :],
                                                 fbuf[:, j2, :]),
                         reads=[B("fbuf")], writes=[B(f"fcv{j2 % 2}")])
                    p.op("scalar",
                         lambda e, j2=j2: e.activation(fcv[:, j2 % 2, 1, :],
                                                       fbuf[:, j2, :], AF.Square),
                         reads=[B("fbuf")], writes=[B(f"fcq{j2 % 2}")])
                    p.op("tensor",
                         lambda e, j2=j2: e.matmul(
                             psSt2[0][32:33, :], onesc[:], fcv[:, j2 % 2, 0, :],
                             start=(j2 == 0), stop=(j2 == NCK - 1)),
                         reads=[B("onesc"), B(f"fcv{j2 % 2}")],
                         writes=[B("psSt20")], tag="stm2")
                    p.op("tensor",
                         lambda e, j2=j2: e.matmul(
                             psSt2[1][0:1, :], onesc[:], fcv[:, j2 % 2, 1, :],
                             start=(j2 == 0), stop=(j2 == NCK - 1)),
                         reads=[B("onesc"), B(f"fcq{j2 % 2}")],
                         writes=[B("psSt21")], tag="stm2")
                p.op("vector",
                     lambda e: e.tensor_scalar_mul(rows[:, 0, :],
                                                   psSt2[0][32:33, :], 1.0 / D),
                     reads=[B("psSt20")], writes=[B("ln_mu")])
                p.op("vector",
                     lambda e: e.tensor_scalar_mul(rows[:, 1, :],
                                                   psSt2[1][0:1, :], 1.0 / D),
                     reads=[B("psSt21")], writes=[B("ln_msq"), B("psSt21")])
                p.op("vector",
                     lambda e: e.scalar_tensor_tensor(
                         rows[:, 2, :], rows[:, 0, :], 1.0, rows[:, 0, :],
                         ALU.mult, ALU.mult),
                     reads=[B("ln_mu")], writes=[B("ln_mu2")])
                p.op("vector",
                     lambda e: e.tensor_sub(rows[:, 1, :], rows[:, 1, :],
                                            rows[:, 2, :]),
                     reads=[B("ln_msq"), B("ln_mu2")], writes=[B("ln_var")])
                p.op("scalar",
                     lambda e: e.activation(rows[:, 2, :], rows[:, 1, :],
                                            AF.Sqrt, bias=s_eps[:]),
                     reads=[B("ln_var"), B("eps")], writes=[B("ln_std")])
                p.op("vector",
                     lambda e: e.reciprocal(rows[:, 3, :], rows[:, 2, :]),
                     reads=[B("ln_std")], writes=[B("ln_r2_row")])
                p.op("vector",
                     lambda e: e.scalar_tensor_tensor(
                         rows[:, 4, :], rows[:, 0, :], -1.0, rows[:, 3, :],
                         ALU.mult, ALU.mult),
                     reads=[B("ln_mu"), B("ln_r2_row")], writes=[B("ln_n2_row")])
                rep_row(3, 0, psRp2[0][:], drepA[:], B("drepA"), "ln_r2")
                rep_row(4, 2, psRp2[1][:], drepB[:], B("drepB"), "ln_n2")
                bq, cq = q // 4, q % 4
                for j2 in range(NCK):
                    jr = j2 % 2
                    p.op("vector",
                         lambda e, j2=j2, jr=jr: e.tensor_mul(
                             outst[:, jr, :], fbuf[:, j2, :], drepA[:]),
                         reads=[B("fbuf"), B("drepA"), B(f"outst{jr}")],
                         writes=[B(f"outw{jr}")], tag="fin1")
                    p.op("vector",
                         lambda e, jr=jr: e.tensor_add(
                             outst[:, jr, :], outst[:, jr, :], drepB[:]),
                         reads=[B(f"outw{jr}"), B("drepB")],
                         writes=[B(f"outw{jr}")], tag="fin2")
                    p.dma("scalar", f"d_out{jr}",
                          lambda e, j2=j2, jr=jr, bq=bq, cq=cq: e.dma_start(
                              out=yt[bq, j2 * 128:(j2 + 1) * 128,
                                     cq * 512:(cq + 1) * 512],
                              in_=outst[:, jr, :]),
                          reads=[B(f"outw{jr}")],
                          writes=[B(f"outst{jr}"), B("ytout")], tag="ostore")

            for fs in ("d_out0", "d_out1"):
                fin = p.dma_sem(fs)
                p.op("sync", lambda e, fin=fin: e.wait_ge(fin.handle, fin.count))

        p.plan()
        p.emit(nc, es)
    return nc, p



_cached = {}


def _rope_tables():
    half = DH // 2
    inv = 1.0 / (10000.0 ** (np.arange(0, DH, 2, dtype=np.float64) / DH))
    t = np.arange(S, dtype=np.float64)
    fr = np.outer(inv, t)
    cos = np.cos(fr)
    sin = np.sin(fr)
    cosf = np.concatenate([cos, cos], 0)
    sinf = np.concatenate([-sin, sin], 0)
    return (np.ascontiguousarray(cosf.astype(np.float32).astype(BF)),
            np.ascontiguousarray(sinf.astype(np.float32).astype(BF)))


def kernel(L0_attn_x, L0_attn_q_w, L0_attn_k_w, L0_attn_v_w, L0_attn_q_b,
           L0_attn_k_b, L0_attn_v_b, L0_attn_out_w, L0_attn_out_b,
           L0_ffn_router_weight, L0_ffn_router_bias, L0_ffn_w1, L0_ffn_b1,
           L0_ffn_w2, L0_ffn_b2):
    n_cores = 8
    if "nc" not in _cached:
        _cached["nc"] = build_program()
    nc, _ = _cached["nc"]

    cosk_t, sink_t = _rope_tables()

    f32 = np.float32
    wqt = np.ascontiguousarray(np.asarray(L0_attn_q_w, f32).T)
    wkt = np.ascontiguousarray(np.asarray(L0_attn_k_w, f32).T)
    wvt = np.ascontiguousarray(np.asarray(L0_attn_v_w, f32).T)
    wot = np.ascontiguousarray(np.asarray(L0_attn_out_w, f32).T)
    w1 = np.ascontiguousarray(np.asarray(L0_ffn_w1, f32))
    w2 = np.ascontiguousarray(np.asarray(L0_ffn_w2, f32))
    qb2 = np.ascontiguousarray(np.asarray(L0_attn_q_b, f32).reshape(NCK, 128).T)
    kb2 = np.ascontiguousarray(np.asarray(L0_attn_k_b, f32).reshape(NCK, 128).T)
    vbr = np.ascontiguousarray(np.broadcast_to(np.asarray(L0_attn_v_b, f32).reshape(1, D), (128, D)))
    ob2 = np.ascontiguousarray(np.asarray(L0_attn_out_b, f32).reshape(NCK, 128).T)
    b1c = np.ascontiguousarray(np.asarray(L0_ffn_b1, f32).reshape(64, 128).T)
    b2c = np.ascontiguousarray(np.asarray(L0_ffn_b2, f32).reshape(NCK, 128).T)
    rwc = np.ascontiguousarray(
        np.asarray(L0_ffn_router_weight, f32).reshape(NCK, 128).T)
    rbx = np.asarray(L0_ffn_router_bias, f32).reshape(1, 1)
    idf = np.eye(128, dtype=f32)
    onesc = np.ones((128, 1), BF)
    onesr = np.ones((1, 128), BF)
    ones1f = np.ones((1, 128), f32)

    x = np.asarray(L0_attn_x, f32)
    in_maps = []
    for core in range(n_cores):
        xs = x[core * BL:(core + 1) * BL]
        xtin = np.ascontiguousarray(xs.transpose(0, 2, 1))
        in_maps.append(dict(
            xtin=xtin, wqt=wqt, wkt=wkt, wvt=wvt, wot=wot, w1=w1, w2=w2,
            qb2=qb2, kb2=kb2, vbr=vbr, ob2=ob2, b1c=b1c, b2c=b2c, rwc=rwc,
            rbx=rbx, cosk=cosk_t, sink=sink_t,
            idf=idf, onesc=onesc, onesr=onesr, ones1f=ones1f,
        ))

    res = run_bass_kernel_spmd(nc, in_maps, core_ids=list(range(n_cores)))
    out = np.empty((16, S, D), np.float32)
    for core in range(n_cores):
        ytc = res.results[core]["yt"]
        out[core * BL:(core + 1) * BL] = ytc.transpose(0, 2, 1)
    return out


# version 04
# speedup vs baseline: 1.0494x; 1.0494x over previous
import numpy as np
import ml_dtypes
from collections import defaultdict
from contextlib import ExitStack

import os
import concourse.bass as bass
import concourse.mybir as mybir
from concourse.bass_utils import run_bass_kernel_spmd

F32 = mybir.dt.float32
BF16 = mybir.dt.bfloat16
AF = mybir.ActivationFunctionType
ALU = mybir.AluOpType
AX = mybir.AxisListType
BF = ml_dtypes.bfloat16

ENGINES = ("tensor", "vector", "scalar", "gpsimd", "sync")

D = 2048
S = 2048
H = 16
DH = 128
DFF = 8192
NCK = 16
BL = 2
TOK = BL * S



class Buf:
    __slots__ = ("name", "writer", "readers")

    def __init__(self, name):
        self.name = name
        self.writer = None
        self.readers = []


class DmaSem:
    __slots__ = ("name", "count", "handle", "final_wait")

    def __init__(self, name, final_wait=False):
        self.name = name
        self.count = 0
        self.handle = None
        self.final_wait = final_wait


class _Op:
    __slots__ = ("engine", "emit_fn", "reads", "writes", "dma_sem", "dma_cum",
                 "inc_needed", "inc_count", "waits", "tag", "extra_dma_waits",
                 "barrier_state")

    def __init__(self, engine, emit_fn, reads, writes, dma_sem, tag):
        self.engine = engine
        self.emit_fn = emit_fn
        self.reads = list(reads)
        self.writes = list(writes)
        self.dma_sem = dma_sem
        self.dma_cum = 0
        self.inc_needed = False
        self.inc_count = 0
        self.waits = []
        self.tag = tag
        self.extra_dma_waits = None
        self.barrier_state = None


class Prog:
    def __init__(self):
        self.ops = []
        self.dma_sems = {}
        self._bufs = {}
        self._last_op = {}

    def buf(self, name):
        b = self._bufs.get(name)
        if b is None:
            b = self._bufs[name] = Buf(name)
        return b

    def dma_sem(self, name):
        s = self.dma_sems.get(name)
        if s is None:
            s = self.dma_sems[name] = DmaSem(
                name, final_wait=name.startswith(("cst", "prep")))
        return s

    def op(self, engine, emit_fn, reads=(), writes=(), dma_sem=None, tag=""):
        o = _Op(engine, emit_fn, reads, writes, dma_sem, tag)
        if dma_sem is not None:
            dma_sem.count += 16
            o.dma_cum = dma_sem.count
        self.ops.append(o)
        self._last_op[engine] = o
        return o

    def dma(self, engine, sem_name, emit_fn, reads=(), writes=(), tag=""):
        return self.op(engine, emit_fn, reads, writes, self.dma_sem(sem_name), tag)

    def barrier(self, tag="bar"):
        o = _Op("sync", None, (), (), None, f"{tag}S")
        o.barrier_state = (dict(self._last_op),
                           [(sm, sm.count) for sm in self.dma_sems.values()])
        self.ops.append(o)

    def plan(self):
        for b in self._bufs.values():
            b.writer = None
            b.readers = []
        deps = [set() for _ in self.ops]
        idx = {}
        for i, o in enumerate(self.ops):
            idx[id(o)] = i
        for i, o in enumerate(self.ops):
            if o.barrier_state is not None:
                for b in self._bufs.values():
                    b.writer = None
                    b.readers = []
                continue
            for b in o.reads:
                w = b.writer
                if w is not None and w is not o:
                    deps[i].add(w)
            for b in o.writes:
                w = b.writer
                if w is not None and w is not o:
                    same_group = (o.dma_sem is not None
                                  and w.dma_sem is o.dma_sem and not b.readers)
                    if not same_group:
                        deps[i].add(w)
                for r in b.readers:
                    if r is not o:
                        deps[i].add(r)
            for b in o.reads:
                b.readers.append(o)
            for b in o.writes:
                b.writer = o
                b.readers = []

        eng_ops = defaultdict(list)
        for o in self.ops:
            if o.barrier_state is None:
                eng_ops[o.engine].append(o)

        op_knowledge = {}
        eng_know = {e: {} for e in ENGINES}
        wait_edges = [[] for _ in self.ops]
        cur_barrier = None
        for i, o in enumerate(self.ops):
            if o.barrier_state is not None:
                cur_barrier = o.barrier_state
                continue
            know = eng_know[o.engine]
            best = {}
            dd = set(deps[i])
            if cur_barrier is not None:
                last_ops, _ = cur_barrier
                for e2, w2 in last_ops.items():
                    if w2 is not None:
                        dd.add(w2)
            for w in sorted(dd, key=lambda x: idx[id(x)]):
                if w.engine == "tensor" and o.engine == "tensor" and w.dma_sem is None:
                    continue
                if w.dma_sem is not None:
                    key = ("dma", w.dma_sem.name)
                    val = w.dma_sem.count if w.dma_sem.final_wait else w.dma_cum
                    cur = know.get(key)
                    if cur is not None and cur >= val:
                        continue
                    if key not in best or best[key][0] < val:
                        best[key] = (val, w)
                    know[key] = val
                else:
                    key = ("eng", w.engine)
                    wpos = idx[id(w)]
                    cur = know.get(key)
                    if cur is not None and cur >= wpos:
                        continue
                    if key not in best or best[key][0] < wpos:
                        best[key] = (wpos, w)
                    know[key] = wpos
                pk = op_knowledge.get(id(w))
                if pk:
                    for k2, v2 in pk.items():
                        if know.get(k2, -1) < v2:
                            know[k2] = v2
            if cur_barrier is not None:
                for sem, cnt in cur_barrier[1]:
                    key = ("dma", sem.name)
                    if know.get(key, -1) < cnt:
                        if key not in best or best[key][0] < cnt:
                            best[key] = (cnt, None)
                        know[key] = cnt
            if o.extra_dma_waits:
                for sem, cnt in o.extra_dma_waits:
                    key = ("dma", sem.name)
                    if know.get(key, -1) < cnt:
                        if key not in best or best[key][0] < cnt:
                            best[key] = (cnt, None)
                        know[key] = cnt
            wait_edges[i] = [(k, v, w) for k, (v, w) in best.items()]
            op_knowledge[id(o)] = dict(know)

        for i, o in enumerate(self.ops):
            for key, val, w in wait_edges[i]:
                if key[0] == "eng":
                    w.inc_needed = True
        inc_counts = defaultdict(int)
        for o in self.ops:
            if o.dma_sem is None and o.inc_needed:
                inc_counts[o.engine] += 1
                o.inc_count = inc_counts[o.engine]
        for i, o in enumerate(self.ops):
            best = {}
            for key, val, w in wait_edges[i]:
                v = w.inc_count if (key[0] == "eng") else val
                if best.get(key, -1) < v:
                    best[key] = v
            o.waits = sorted(best.items(), key=lambda kv: (kv[0][0], str(kv[0][1])))
        self._eng_ops = eng_ops
        return self

    def emit(self, nc, es):
        eng_sems = {}
        for e in ENGINES:
            if any(o.inc_needed for o in self._eng_ops.get(e, ())):
                eng_sems[e] = es.enter_context(nc.semaphore(f"es_{e}"))
        for name, dssem in self.dma_sems.items():
            dssem.handle = es.enter_context(nc.semaphore(f"ds_{name}"))

        def sem_handle(key):
            return eng_sems[key[1]] if key[0] == "eng" else self.dma_sems[key[1]].handle

        block = es.enter_context(nc.Block())

        def make_body(e):
            def body(eng):
                for o in self._eng_ops.get(e, ()):
                    for key, v in o.waits:
                        eng.wait_ge(sem_handle(key), v)
                    ins = o.emit_fn(eng)
                    if o.dma_sem is not None:
                        ins.then_inc(o.dma_sem.handle, 16)
                    elif o.inc_needed:
                        ins.then_inc(eng_sems[o.engine], 1)
            return body

        for e in ENGINES:
            if self._eng_ops.get(e):
                getattr(block, e)(make_body(e))

    def stats(self):
        from collections import Counter
        c = Counter(o.engine for o in self.ops)
        return (f"ops={dict(c)} total={len(self.ops)} "
                f"waits={sum(len(o.waits) for o in self.ops)} "
                f"incs={sum(1 for o in self.ops if o.inc_needed)} "
                f"dma_sems={len(self.dma_sems)}")



def build_program():
    PH = os.environ.get("MK_PHASES", "ABCD")
    PREP = os.environ.get("MK_PREP", "all")
    ABITS = os.environ.get("MK_ABITS", "full")
    AJ = int(os.environ.get("MK_AJ", "16"))
    ABATCH = int(os.environ.get("MK_ABATCH", str(BL)))
    nc = bass.Bass("TRN2", target_bir_lowering=False)

    xtin = nc.dram_tensor("xtin", [BL, D, S], F32, kind="ExternalInput").ap()
    wqt = nc.dram_tensor("wqt", [D, D], F32, kind="ExternalInput").ap()
    wkt = nc.dram_tensor("wkt", [D, D], F32, kind="ExternalInput").ap()
    wvt = nc.dram_tensor("wvt", [D, D], F32, kind="ExternalInput").ap()
    wot = nc.dram_tensor("wot", [D, D], F32, kind="ExternalInput").ap()
    w1 = nc.dram_tensor("w1", [D, DFF], F32, kind="ExternalInput").ap()
    w2 = nc.dram_tensor("w2", [DFF, D], F32, kind="ExternalInput").ap()
    qb2 = nc.dram_tensor("qb2", [128, NCK], F32, kind="ExternalInput").ap()
    kb2 = nc.dram_tensor("kb2", [128, NCK], F32, kind="ExternalInput").ap()
    vbr = nc.dram_tensor("vbr", [128, D], F32, kind="ExternalInput").ap()
    ob2 = nc.dram_tensor("ob2", [128, NCK], F32, kind="ExternalInput").ap()
    b1c = nc.dram_tensor("b1c", [128, 64], F32, kind="ExternalInput").ap()
    b2c = nc.dram_tensor("b2c", [128, NCK], F32, kind="ExternalInput").ap()
    rwc = nc.dram_tensor("rwc", [128, NCK], F32, kind="ExternalInput").ap()
    rbx = nc.dram_tensor("rbx", [1, 1], F32, kind="ExternalInput").ap()
    cosk = nc.dram_tensor("cosk", [128, S], BF16, kind="ExternalInput").ap()
    sink = nc.dram_tensor("sink", [128, S], BF16, kind="ExternalInput").ap()
    idfh = nc.dram_tensor("idf", [128, 128], F32, kind="ExternalInput").ap()
    onesch = nc.dram_tensor("onesc", [128, 1], BF16, kind="ExternalInput").ap()
    onesrh = nc.dram_tensor("onesr", [1, 128], BF16, kind="ExternalInput").ap()
    ones1fh = nc.dram_tensor("ones1f", [1, 128], F32, kind="ExternalInput").ap()

    yt = nc.dram_tensor("yt", [BL, D, S], F32, kind="ExternalOutput").ap()

    wqt_b = nc.dram_tensor("wqt_b", [D, D], BF16, kind="Internal").ap()
    wkt_b = nc.dram_tensor("wkt_b", [D, D], BF16, kind="Internal").ap()
    wvt_b = nc.dram_tensor("wvt_b", [D, D], BF16, kind="Internal").ap()
    wot_b = nc.dram_tensor("wot_b", [D, D], BF16, kind="Internal").ap()
    w1_b = nc.dram_tensor("w1_b", [D, DFF], BF16, kind="Internal").ap()
    w2_b = nc.dram_tensor("w2_b", [DFF, D], BF16, kind="Internal").ap()
    qt_s = nc.dram_tensor("qt_s", [BL, H, 128, S], BF16, kind="Internal").ap()
    kt_s = nc.dram_tensor("kt_s", [BL, H, 128, S], BF16, kind="Internal").ap()
    v_s = nc.dram_tensor("v_s", [BL, H, 16, 128, 128], BF16, kind="Internal").ap()
    x1_s = nc.dram_tensor("x1_s", [BL, NCK, 128, S], F32, kind="Internal").ap()
    x1t_b = nc.dram_tensor("x1t_b", [NCK, 128, TOK], BF16, kind="Internal").ap()

    p = Prog()
    B = p.buf
    es = ExitStack()
    with es:
        idf = nc.alloc_sbuf_tensor("c_idf", [128, 128], F32)
        onesc = nc.alloc_sbuf_tensor("c_onesc", [128, 1], BF16)
        onesr = nc.alloc_sbuf_tensor("c_onesr", [1, 128], BF16)
        ones1f = nc.alloc_sbuf_tensor("c_ones1f", [1, 128], F32)
        s_qb = nc.alloc_sbuf_tensor("c_qb", [128, NCK], F32)
        s_kb = nc.alloc_sbuf_tensor("c_kb", [128, NCK], F32)
        s_ob = nc.alloc_sbuf_tensor("c_ob", [128, NCK], F32)
        s_b1 = nc.alloc_sbuf_tensor("c_b1", [128, 64], F32)
        s_b2 = nc.alloc_sbuf_tensor("c_b2", [128, NCK], F32)
        s_rwf = nc.alloc_sbuf_tensor("c_rwf", [128, NCK], F32)
        s_rw = nc.alloc_sbuf_tensor("c_rw", [128, NCK], BF16)
        s_rb = nc.alloc_sbuf_tensor("c_rb", [1, 1], F32)
        s_rbh = nc.alloc_sbuf_tensor("c_rbh", [1, 1], F32)
        s_qbs = nc.alloc_sbuf_tensor("c_qbs", [128, NCK], F32)
        vbrep = nc.alloc_sbuf_tensor("c_vbrep", [128, D], F32)
        negshift = nc.alloc_sbuf_tensor("c_negsh", [128, 2], F32)
        maxcol = nc.alloc_sbuf_tensor("c_maxcol", [128, 1], F32)
        negsh1 = nc.alloc_sbuf_tensor("c_negsh1", [1, 1], F32)
        lrepA = nc.alloc_sbuf_tensor("c_lrepA", [128, 512], F32)
        lrepB = nc.alloc_sbuf_tensor("c_lrepB", [128, 512], F32)
        cx2st = nc.alloc_sbuf_tensor("c_x2st", [128, 2, 512], F32)
        cx1tst = nc.alloc_sbuf_tensor("c_x1tst", [128, 2, 512], BF16)
        chcv = nc.alloc_sbuf_tensor("c_hcv", [128, 3, 2, 512], BF16)
        cxcf = nc.alloc_sbuf_tensor("c_xcf", [128, 2, 512], F32)
        rows = nc.alloc_sbuf_tensor("c_rows", [1, 7, 512], F32)
        rowsb = nc.alloc_sbuf_tensor("c_rowsb", [1, 6, 512], BF16)
        s_eps = nc.alloc_sbuf_tensor("c_eps", [1, 1], F32)

        for ap_h, t, nm in ((idfh, idf, "idf"), (onesch, onesc, "onesc"),
                            (onesrh, onesr, "onesr"), (ones1fh, ones1f, "ones1f"),
                            (qb2, s_qb, "qb"), (kb2, s_kb, "kb"),
                            (ob2, s_ob, "ob"), (b1c, s_b1, "b1"),
                            (b2c, s_b2, "b2"), (rbx, s_rb, "rb")):
            p.dma("sync", "cst", lambda e, a=ap_h, t=t: e.dma_start(out=t[:], in_=a),
                  writes=[B(nm)])
        p.dma("sync", "cst",
              lambda e: e.dma_start(out=s_rwf[:], in_=rwc), writes=[B("rwf")])
        p.op("vector", lambda e: e.tensor_copy(s_rw[:], s_rwf[:]),
             reads=[B("rwf")], writes=[B("rw")])
        p.op("scalar", lambda e: e.mul(s_rbh[:], s_rb[:], 0.5),
             reads=[B("rb")], writes=[B("rbh")])
        p.op("scalar", lambda e: e.mul(s_qbs[:], s_qb[:], DH ** -0.5),
             reads=[B("qb")], writes=[B("qbs")])
        p.op("vector", lambda e: e.memset(negshift[:], 0.0),
             writes=[B("sh0"), B("sh1")])
        p.op("vector", lambda e: e.memset(s_eps[:], 1e-5), writes=[B("eps")])

        def cast_w(src, dst, rows_total, cols, nm, chunk=128):
            for r0 in range(0, rows_total, chunk):
                p.dma("gpsimd", f"prep_{nm}",
                      lambda e, s=src[r0:r0 + chunk, :], d=dst[r0:r0 + chunk, :]:
                      e.dma_start(out=d, in_=s),
                      writes=[B(nm)], tag=f"cast_{nm}")
        if PREP in ("all", "casts"):
            cast_w(wqt, wqt_b, D, D, "wqt_b")
            cast_w(wkt, wkt_b, D, D, "wkt_b")
            cast_w(wvt, wvt_b, D, D, "wvt_b")
            cast_w(wot, wot_b, D, D, "wot_b")
            cast_w(w1, w1_b, D, DFF, "w1_b")
            cast_w(w2, w2_b, DFF, D, "w2_b")

        p.dma("sync", "cst", lambda e: e.dma_start(out=vbrep[:], in_=vbr),
              writes=[B("vbrep")])
        if PREP == "bar":
            p.barrier("pvb0")

        def rep_row(slot, sb, ps, dest, dest_buf, pref):
            hi_b = B(f"{pref}_hi")
            lo_b = B(f"{pref}_lo")
            p.op("scalar",
                 lambda e: e.copy(rowsb[:, sb, :], rows[:, slot, :]),
                 reads=[B(f"{pref}_row")], writes=[hi_b])
            p.op("vector",
                 lambda e: e.tensor_sub(rowsb[:, sb + 1, :], rows[:, slot, :],
                                        rowsb[:, sb, :]),
                 reads=[B(f"{pref}_row"), hi_b], writes=[lo_b])
            p.op("tensor",
                 lambda e: e.matmul(ps, onesr[:], rowsb[:, sb, :],
                                    start=True, stop=False),
                 reads=[B("onesr"), hi_b], writes=[B(f"{pref}_ps")])
            p.op("tensor",
                 lambda e: e.matmul(ps, onesr[:], rowsb[:, sb + 1, :],
                                    start=False, stop=True),
                 reads=[B("onesr"), lo_b], writes=[B(f"{pref}_ps")])
            p.op("vector",
                 lambda e: e.tensor_copy(dest, ps),
                 reads=[B(f"{pref}_ps")], writes=[dest_buf])

        for b in range(min(BL, ABATCH) if "A" in PH else 0):
            with ExitStack() as ea:
                XT = ea.enter_context(nc.sbuf_tensor(f"a_xt{b}", [128, NCK, S], BF16))
                t_cosk = ea.enter_context(nc.sbuf_tensor(f"a_cosk{b}", [128, S], BF16))
                t_sink = ea.enter_context(nc.sbuf_tensor(f"a_sink{b}", [128, S], BF16))
                for ap_h, t, nm in ((cosk, t_cosk, "cosk"), (sink, t_sink, "sink")):
                    p.dma("sync", f"cstA{b}",
                          lambda e, a=ap_h, t=t: e.dma_start(out=t[:], in_=a),
                          writes=[B(nm)])
                xtf = ea.enter_context(nc.sbuf_tensor(f"a_xtf{b}", [128, 2, 1024], F32))
                wqk = ea.enter_context(nc.sbuf_tensor(f"a_wqk{b}", [128, 2, 2, NCK, 128], BF16))
                wvb = ea.enter_context(nc.sbuf_tensor(f"a_wv{b}", [128, 2, NCK, 512], BF16))
                abuf = ea.enter_context(nc.sbuf_tensor(f"a_ab{b}", [128, 3, 512], BF16))
                asw = ea.enter_context(nc.sbuf_tensor(f"a_as{b}", [128, 3, 512], BF16))
                tt1 = ea.enter_context(nc.sbuf_tensor(f"a_t1{b}", [128, 2, 512], BF16))
                tt2 = ea.enter_context(nc.sbuf_tensor(f"a_t2{b}", [128, 2, 512], BF16))
                qst = ea.enter_context(nc.sbuf_tensor(f"a_qst{b}", [128, 2, S], BF16))
                vst = ea.enter_context(nc.sbuf_tensor(f"a_vst{b}", [128, 2, 512], BF16))
                psA = [ea.enter_context(nc.psum_tensor(f"a_psA{b}{r}", [128, 512], F32))
                       for r in range(2)]
                psV = [ea.enter_context(nc.psum_tensor(f"a_psV{b}{r}", [128, 512], F32))
                       for r in range(2)]

                for ch in range(NCK * 2):
                    c, hf = ch // 2, ch % 2
                    p.dma("sync", f"a_x{ch % 2}",
                          lambda e, c=c, hf=hf, ch=ch, b=b: e.dma_start(
                              out=xtf[:, ch % 2, :],
                              in_=xtin[b, c * 128:(c + 1) * 128,
                                       hf * 1024:(hf + 1) * 1024]),
                          reads=[B(f"xtf{ch % 2}")], writes=[B(f"xtfw{ch % 2}")])
                    p.op("scalar",
                         lambda e, c=c, hf=hf, ch=ch: e.copy(
                             XT[:, c, hf * 1024:(hf + 1) * 1024], xtf[:, ch % 2, :]),
                         reads=[B(f"xtfw{ch % 2}")],
                         writes=[B("XT"), B(f"xtf{ch % 2}")], tag="xtcv")

                def wq_load(w, j):
                    src = (wqt_b if w == 0 else wkt_b)
                    nm = ("wqt_b" if w == 0 else "wkt_b")
                    rr = src[:, j * 128:(j + 1) * 128].rearrange(
                        "(c p) n -> p c n", p=128)
                    p.dma("sync", f"a_w{w}{j % 2}",
                          lambda e, rr=rr, w=w, j=j: e.dma_start(
                              out=wqk[:, w, j % 2, :, :], in_=rr),
                          reads=[B(nm), B(f"wqk{w}{j % 2}")],
                          writes=[B(f"wqkw{w}{j % 2}")])

                def qk_block(w, j):
                    tabc = t_cosk
                    tabs = t_sink
                    bias = s_qbs if w == 0 else s_kb
                    scl = DH ** -0.5 if w == 0 else 1.0
                    wb = B(f"wqkw{w}{j % 2}")
                    for c in range(4):
                        sr = (j * 4 + c) % 2
                        ar = (j * 4 + c) % 3
                        sl = slice(c * 512, (c + 1) * 512)
                        for ck in range(NCK):
                            p.op("tensor",
                                 lambda e, w=w, j=j, ck=ck, sl=sl, sr=sr: e.matmul(
                                     psA[sr][:], wqk[:, w, j % 2, ck, :],
                                     XT[:, ck, sl],
                                     start=(ck == 0), stop=(ck == NCK - 1)),
                                 reads=[wb, B("XT")],
                                 writes=[B(f"psA{sr}")], tag="qkmm")
                        if ABITS == "xqm":
                            continue
                        if ABITS == "xqc":
                            p.op("scalar",
                                 lambda e, sr=sr, ar=ar: e.copy(
                                     abuf[:, ar, :], psA[sr][:]),
                                 reads=[B(f"psA{sr}")],
                                 writes=[B(f"ab{ar}"), B(f"psA{sr}")], tag="qkev")
                        else:
                            p.op("scalar",
                                 lambda e, j=j, sr=sr, ar=ar, bias=bias, scl=scl: e.activation(
                                     abuf[:, ar, :], psA[sr][:], AF.Identity,
                                     bias=bias[:, j:j + 1], scale=scl),
                                 reads=[B(f"psA{sr}"), B("qbs" if w == 0 else "kb")],
                                 writes=[B(f"ab{ar}"), B(f"psA{sr}")], tag="qkev")
                        if ABITS in ("xq", "xqm", "xqc"):
                            continue
                        p.dma("scalar", f"a_sw{ar}",
                              lambda e, ar=ar: e.dma_start(out=asw[0:64, ar, :],
                                                           in_=abuf[64:128, ar, :]),
                              reads=[B(f"ab{ar}")], writes=[B(f"as{ar}")])
                        p.dma("scalar", f"a_sw{ar}",
                              lambda e, ar=ar: e.dma_start(out=asw[64:128, ar, :],
                                                           in_=abuf[0:64, ar, :]),
                              reads=[B(f"ab{ar}")], writes=[B(f"as{ar}")])
                        tr = (j * 4 + c) % 2
                        tcn = "cosk"
                        tsn = "sink"
                        p.op("vector",
                             lambda e, ar=ar, tr=tr, sl=sl, tabc=tabc: e.tensor_mul(
                                 tt1[:, tr, :], abuf[:, ar, :], tabc[:, sl]),
                             reads=[B(f"ab{ar}"), B(tcn)],
                             writes=[B(f"tt1{tr}")], tag="rope1")
                        p.op("vector",
                             lambda e, ar=ar, tr=tr, sl=sl, tabs=tabs: e.tensor_mul(
                                 tt2[:, tr, :], asw[:, ar, :], tabs[:, sl]),
                             reads=[B(f"as{ar}"), B(tsn)],
                             writes=[B(f"tt2{tr}"), B(f"ab{ar}")], tag="rope2")
                        p.op("vector",
                             lambda e, tr=tr, sl=sl, j=j: e.tensor_add(
                                 qst[:, j % 2, sl], tt1[:, tr, :], tt2[:, tr, :]),
                             reads=[B(f"tt1{tr}"), B(f"tt2{tr}")],
                             writes=[B(f"qst{j % 2}"), B(f"as{(j * 4 + c) % 3}")],
                             tag="rope3")
                    if ABITS in ("xq", "xqm", "xqc", "xqr"):
                        return
                    dst = (qt_s if w == 0 else kt_s)[b, j]
                    p.dma("scalar", f"a_sp{w}{j % 2}",
                          lambda e, dst=dst, j=j: e.dma_start(out=dst,
                                                              in_=qst[:, j % 2, :]),
                          reads=[B(f"qst{j % 2}")], writes=[B(f"qkspill{w}_{j}")])

                wq_load(0, 0)
                wq_load(1, 0)
                for j in range(min(H, AJ) if ABITS != "x" else 0):
                    if j + 1 < min(H, AJ):
                        wq_load(0, j + 1)
                        wq_load(1, j + 1)
                    qk_block(0, j)
                    qk_block(1, j)

                def wv_load(c4):
                    rr = wvt_b[:, c4 * 512:(c4 + 1) * 512].rearrange(
                        "(c p) n -> p c n", p=128)
                    p.dma("sync", f"a_wv{c4 % 2}",
                          lambda e, rr=rr, c4=c4: e.dma_start(
                              out=wvb[:, c4 % 2, :, :], in_=rr),
                          reads=[B("wvt_b"), B(f"wvb{c4 % 2}")],
                          writes=[B(f"wvbw{c4 % 2}")])

                wv_load(0)
                for c4 in range(4 if ABITS == "full" else 0):
                    if c4 + 1 < 4:
                        wv_load(c4 + 1)
                    dsl = slice(c4 * 512, (c4 + 1) * 512)
                    for t in range(16):
                        vr = (c4 * 16 + t) % 2
                        for ck in range(NCK):
                            p.op("tensor",
                                 lambda e, ck=ck, t=t, c4=c4, vr=vr: e.matmul(
                                     psV[vr][:], XT[:, ck, t * 128:(t + 1) * 128],
                                     wvb[:, c4 % 2, ck, :],
                                     start=(ck == 0), stop=(ck == NCK - 1)),
                                 reads=[B(f"wvbw{c4 % 2}"), B("XT")],
                                 writes=[B(f"psV{vr}")], tag="vmm")
                        p.op("vector",
                             lambda e, vr=vr, dsl=dsl: e.scalar_tensor_tensor(
                                 vst[:, vr, :], psV[vr][:], 1.0, vbrep[:, dsl],
                                 ALU.mult, ALU.add),
                             reads=[B(f"psV{vr}"), B("vbrep")],
                             writes=[B(f"vst{vr}"), B(f"psV{vr}")], tag="vev")
                        for hh in range(4):
                            h = c4 * 4 + hh
                            p.dma("scalar", f"a_spv{vr}",
                                  lambda e, h=h, t=t, vr=vr, hh=hh, b=b: e.dma_start(
                                      out=v_s[b, h, t],
                                      in_=vst[:, vr, hh * 128:(hh + 1) * 128]),
                                  reads=[B(f"vst{vr}")], writes=[B(f"vspill{h}_{t}")])
                p.barrier(f"ab{b}")

            if "B" not in PH:
                continue
            with ExitStack() as eo:
                OT = eo.enter_context(nc.sbuf_tensor(f"b_ot{b}", [128, NCK, S], BF16))

                with ExitStack() as eb:
                    qtb = eb.enter_context(nc.sbuf_tensor(f"b_qt{b}", [128, 2, S], BF16))
                    ktb = eb.enter_context(nc.sbuf_tensor(f"b_kt{b}", [128, 2, S], BF16))
                    vtb = eb.enter_context(nc.sbuf_tensor(f"b_vt{b}", [128, 2, 16, 128], BF16))
                    pt = eb.enter_context(nc.sbuf_tensor(f"b_pt{b}", [128, 5, 1024], BF16))
                    lrec = eb.enter_context(nc.sbuf_tensor(f"b_rc{b}", [1, 2, 512], F32))
                    lrhi = eb.enter_context(nc.sbuf_tensor(f"b_rh{b}", [1, 2, 512], BF16))
                    lrlo = eb.enter_context(nc.sbuf_tensor(f"b_rl{b}", [1, 2, 512], BF16))
                    reps = cxcf
                    psS = [eb.enter_context(
                        nc.psum_tensor(f"b_psS{b}{r}", [128, 1024], F32))
                        for r in range(2)]
                    psO = [eb.enter_context(
                        nc.psum_tensor(f"b_psO{b}{r}", [128, 512], F32))
                        for r in range(2)]
                    psR = [eb.enter_context(
                        nc.psum_tensor(f"b_psR{b}{r}", [128, 512], F32))
                        for r in range(2)]
                    psD = [psR[r][0:1, :] for r in range(2)]
                    psT = psS[0][0:1, 0:128]
                    psB_ = psS[0][0:128, 512 + 256:512 + 257]

                    def head_load(h):
                        hr = h % 2
                        p.dma("sync", f"b_hq{hr}",
                              lambda e, h=h, hr=hr, b=b: e.dma_start(
                                  out=qtb[:, hr, :], in_=qt_s[b, h]),
                              reads=[B(f"qkspill0_{h}"), B(f"qtb{hr}")],
                              writes=[B(f"qtbw{hr}")])
                        p.dma("sync", f"b_hk{hr}",
                              lambda e, h=h, hr=hr, b=b: e.dma_start(
                                  out=ktb[:, hr, :], in_=kt_s[b, h]),
                              reads=[B(f"qkspill1_{h}"), B(f"ktb{hr}")],
                              writes=[B(f"ktbw{hr}")])
                        rr = v_s[b, h].rearrange("t p d -> p t d")
                        p.dma("sync", f"b_hv{hr}",
                              lambda e, rr=rr, hr=hr: e.dma_start(
                                  out=vtb[:, hr, :, :], in_=rr),
                              reads=[B(f"vtb{hr}")] + [B(f"vspill{h}_{t}") for t in range(16)],
                              writes=[B(f"vtbw{hr}")])

                    def b_scores(h, c, g):
                        hr = h % 2
                        sr = g % 2
                        for hf in range(2):
                            j = 2 * g + hf
                            p.op("tensor",
                                 lambda e, j=j, sr=sr, hf=hf, c=c, hr=hr: e.matmul(
                                     psS[sr][:, hf * 512:(hf + 1) * 512],
                                     ktb[:, hr, j * 128:(j + 1) * 128],
                                     qtb[:, hr, c * 512:(c + 1) * 512],
                                     start=True, stop=True),
                                 reads=[B(f"ktbw{hr}"), B(f"qtbw{hr}")],
                                 writes=[B(f"psS{sr}")], tag="sc")

                    def b_shift_chain(h):
                        w = (h + 1) % 2
                        p.op("vector",
                             lambda e: e.reduce_max(maxcol[:], psS[0][:, 0:512],
                                                    AX.X, negate=True),
                             reads=[B("psS0")], writes=[B("maxcol")])
                        p.op("tensor", lambda e: e.transpose(psT, maxcol[:], idf[:]),
                             reads=[B("maxcol"), B("idf")], writes=[B("psS0")])
                        p.op("vector",
                             lambda e: e.tensor_reduce(negsh1[:], psT, AX.X, ALU.min),
                             reads=[B("psS0")], writes=[B("negsh1")])
                        p.op("tensor",
                             lambda e: e.matmul(psB_, ones1f[:], negsh1[:],
                                                start=True, stop=True),
                             reads=[B("negsh1"), B("ones1f")], writes=[B("psS0")])
                        p.op("scalar",
                             lambda e, w=w: e.copy(negshift[:, w:w + 1], psB_),
                             reads=[B("psS0")], writes=[B(f"sh{w}")])

                    def b_pv_dsum(h, c, g):
                        hr = h % 2
                        oc = c % 2
                        pr = g % 5
                        for hf in range(2):
                            j = 2 * g + hf
                            p.op("tensor",
                                 lambda e, j=j, oc=oc, pr=pr, hf=hf, hr=hr: e.matmul(
                                     psO[oc][:], vtb[:, hr, j, :],
                                     pt[:, pr, hf * 512:(hf + 1) * 512],
                                     start=(j == 0), stop=(j == 15)),
                                 reads=[B(f"vtbw{hr}"), B(f"pt{pr}")],
                                 writes=[B(f"psO{oc}")], tag="pv")
                            p.op("tensor",
                                 lambda e, j=j, m=c % 2, pr=pr, hf=hf: e.matmul(
                                     psD[m], onesc[:],
                                     pt[:, pr, hf * 512:(hf + 1) * 512],
                                     start=(j == 0), stop=(j == 15)),
                                 reads=[B("onesc"), B(f"pt{pr}")],
                                 writes=[B(f"bkDR{c % 2}")], tag="ds")

                    def b_epi_scalar(c):
                        m = c % 2
                        p.op("vector",
                             lambda e, m=m: e.reciprocal(lrec[:, m, :], psD[m]),
                             reads=[B(f"bkDR{m}")], writes=[B(f"lrec{m}")])
                        p.op("scalar",
                             lambda e, m=m: e.copy(lrhi[:, m, :], lrec[:, m, :]),
                             reads=[B(f"lrec{m}")], writes=[B(f"lrhi{m}")])
                        p.op("vector",
                             lambda e, m=m: e.tensor_sub(lrlo[:, m, :],
                                                         lrec[:, m, :],
                                                         lrhi[:, m, :]),
                             reads=[B(f"lrec{m}"), B(f"lrhi{m}")],
                             writes=[B(f"lrlo{m}")])

                    def b_epi_pe(h, c):
                        m = c % 2
                        p.op("tensor",
                             lambda e, m=m: e.matmul(psR[m][:], onesr[:],
                                                     lrhi[:, m, :],
                                                     start=True, stop=False),
                             reads=[B("onesr"), B(f"lrhi{m}")],
                             writes=[B(f"bkDR{m}")])
                        p.op("tensor",
                             lambda e, m=m: e.matmul(psR[m][:], onesr[:],
                                                     lrlo[:, m, :],
                                                     start=False, stop=True),
                             reads=[B("onesr"), B(f"lrlo{m}")],
                             writes=[B(f"bkDR{m}")])
                        p.op("vector",
                             lambda e, m=m: e.tensor_copy(reps[:, m, :], psR[m][:]),
                             reads=[B(f"bkDR{m}")], writes=[B(f"reps{m}")])
                        p.op("vector",
                             lambda e, h=h, c=c, m=m: e.tensor_mul(
                                 OT[:, h, c * 512:(c + 1) * 512], psO[m][:],
                                 reps[:, m, :]),
                             reads=[B(f"psO{m}"), B(f"reps{m}")],
                             writes=[B("OT")], tag="oev")

                    LAG = 4
                    head_load(0)
                    pend = [None]
                    for h in range(H):
                        if h + 1 < H:
                            head_load(h + 1)
                        rd = h % 2
                        for c in range(4):
                            b_scores(h, c, 0)
                            for g in range(8):
                                sr = g % 2
                                pr = g % 5
                                if g + 1 < 8:
                                    b_scores(h, c, g + 1)
                                p.op("scalar",
                                     lambda e, sr=sr, pr=pr, rd=rd: e.activation(
                                         pt[:, pr, :], psS[sr][:], AF.Exp,
                                         bias=negshift[:, rd:rd + 1], scale=1.0),
                                     reads=[B(f"psS{sr}"), B(f"sh{rd}")],
                                     writes=[B(f"pt{pr}"), B(f"psS{sr}")], tag="exp")
                                if g == 0 and c == 3:
                                    b_shift_chain(h)
                                if g >= LAG:
                                    b_pv_dsum(h, c, g - LAG)
                                if g == 5 and pend[0] is not None:
                                    b_epi_pe(*pend[0])
                                    pend[0] = None
                            for g in range(8 - LAG, 8):
                                b_pv_dsum(h, c, g)
                            b_epi_scalar(c)
                            pend[0] = (h, c)
                    b_epi_pe(*pend[0])
                    pend[0] = None
                    p.barrier(f"bb{b}")

                if "C" not in PH:
                    continue
                with ExitStack() as ec:
                    wob = ec.enter_context(nc.sbuf_tensor(f"c_wo{b}", [128, 2, NCK, 128], BF16))
                    hbuf = ec.enter_context(nc.sbuf_tensor(f"c_h{b}", [128, NCK, 512], F32))
                    xcf, hcv, x2st, x1tst = cxcf, chcv, cx2st, cx1tst
                    psY = [ec.enter_context(
                        nc.psum_tensor(f"c_psY{b}{r}", [128, 512], F32))
                        for r in range(2)]
                    psSt = [ec.enter_context(
                        nc.psum_tensor(f"c_psSt{b}{r}", [128, 512], F32))
                        for r in range(2)]
                    psRp = [ec.enter_context(
                        nc.psum_tensor(f"c_psRp{b}{r}", [128, 512], F32))
                        for r in range(2)]

                    def wo_load(c, j):
                        rr = wot_b[:, j * 128:(j + 1) * 128].rearrange(
                            "(c p) n -> p c n", p=128)
                        p.dma("sync", f"c_w{j % 2}",
                              lambda e, rr=rr, j=j: e.dma_start(
                                  out=wob[:, j % 2, :, :], in_=rr),
                              reads=[B("wot_b"), B(f"wob{j % 2}")],
                              writes=[B(f"wobw{j % 2}")])

                    def xc_load(c, j):
                        p.dma("sync", f"c_x{j % 2}",
                              lambda e, c=c, j=j, b=b: e.dma_start(
                                  out=xcf[:, j % 2, :],
                                  in_=xtin[b, j * 128:(j + 1) * 128,
                                           c * 512:(c + 1) * 512]),
                              reads=[B(f"xcf{j % 2}")], writes=[B(f"xcfw{j % 2}")])

                    def emit_stats_c(j):
                        p.op("tensor",
                             lambda e, j=j: e.matmul(
                                 psSt[0][0:1, :], onesc[:], hcv[:, j % 3, 0, :],
                                 start=(j == 0), stop=(j == NCK - 1)),
                             reads=[B("onesc"), B(f"hcv{j % 3}")],
                             writes=[B("psSt0")], tag="stm")
                        p.op("tensor",
                             lambda e, j=j: e.matmul(
                                 psSt[1][0:1, :], onesc[:], hcv[:, j % 3, 1, :],
                                 start=(j == 0), stop=(j == NCK - 1)),
                             reads=[B("onesc"), B(f"hcq{j % 3}")],
                             writes=[B("psSt1")], tag="stm")

                    for c in range(4):
                        csl = slice(c * 512, (c + 1) * 512)
                        wo_load(c, 0)
                        xc_load(c, 0)
                        for j in range(NCK):
                            yr = j % 2
                            if j + 1 < NCK:
                                wo_load(c, j + 1)
                                xc_load(c, j + 1)
                            for ck in range(NCK):
                                p.op("tensor",
                                     lambda e, ck=ck, j=j, csl=csl, yr=yr: e.matmul(
                                         psY[yr][:], wob[:, j % 2, ck, :],
                                         OT[:, ck, csl],
                                         start=(ck == 0), stop=(ck == NCK - 1)),
                                     reads=[B(f"wobw{j % 2}"), B("OT")],
                                     writes=[B(f"psY{yr}")], tag="yomm")
                            p.op("vector",
                                 lambda e, j=j, yr=yr: e.scalar_tensor_tensor(
                                     hbuf[:, j, :], psY[yr][:], s_ob[:, j:j + 1],
                                     xcf[:, j % 2, :], ALU.add, ALU.add),
                                 reads=[B(f"psY{yr}"), B("ob"), B(f"xcfw{j % 2}")],
                                 writes=[B("hbuf"), B(f"psY{yr}"), B(f"xcf{j % 2}")],
                                 tag="resid")
                            p.op("scalar",
                                 lambda e, j=j: e.copy(hcv[:, j % 3, 0, :],
                                                       hbuf[:, j, :]),
                                 reads=[B("hbuf")], writes=[B(f"hcv{j % 3}")],
                                 tag="hcv")
                            p.op("scalar",
                                 lambda e, j=j: e.activation(hcv[:, j % 3, 1, :],
                                                             hbuf[:, j, :],
                                                             AF.Square),
                                 reads=[B("hbuf")], writes=[B(f"hcq{j % 3}")],
                                 tag="hsq")
                            if j >= 2:
                                emit_stats_c(j - 2)
                        for jl in range(NCK - 2, NCK):
                            emit_stats_c(jl)
                        p.op("vector",
                             lambda e: e.tensor_scalar_mul(rows[:, 0, :],
                                                           psSt[0][0:1, :],
                                                           1.0 / D),
                             reads=[B("psSt0")], writes=[B("ln_mu")], tag="mu")
                        p.op("vector",
                             lambda e: e.tensor_scalar_mul(rows[:, 1, :],
                                                           psSt[1][0:1, :],
                                                           1.0 / D),
                             reads=[B("psSt1")], writes=[B("ln_msq"), B("psSt1")])
                        p.op("vector",
                             lambda e: e.scalar_tensor_tensor(
                                 rows[:, 2, :], rows[:, 0, :], 1.0, rows[:, 0, :],
                                 ALU.mult, ALU.mult),
                             reads=[B("ln_mu")], writes=[B("ln_mu2")])
                        p.op("vector",
                             lambda e: e.tensor_sub(rows[:, 1, :], rows[:, 1, :],
                                                    rows[:, 2, :]),
                             reads=[B("ln_msq"), B("ln_mu2")], writes=[B("ln_var")])
                        p.op("scalar",
                             lambda e: e.activation(rows[:, 2, :], rows[:, 1, :],
                                                    AF.Sqrt, bias=s_eps[:]),
                             reads=[B("ln_var"), B("eps")], writes=[B("ln_std")])
                        p.op("vector",
                             lambda e: e.reciprocal(rows[:, 3, :], rows[:, 2, :]),
                             reads=[B("ln_std")], writes=[B("ln_rstd")])
                        p.op("vector",
                             lambda e: e.tensor_scalar_mul(rows[:, 3, :],
                                                           rows[:, 3, :], 2.0),
                             reads=[B("ln_rstd")], writes=[B("ln_r2_row")])
                        p.op("vector",
                             lambda e: e.scalar_tensor_tensor(
                                 rows[:, 4, :], rows[:, 0, :], -1.0, rows[:, 3, :],
                                 ALU.mult, ALU.mult),
                             reads=[B("ln_mu"), B("ln_r2_row")],
                             writes=[B("ln_n2_row")])
                        rep_row(3, 0, psRp[0][:], lrepA[:], B("lnrepA"), "ln_r2")
                        rep_row(4, 2, psRp[1][:], lrepB[:], B("lnrepB"), "ln_n2")
                        for j in range(NCK):
                            jr = j % 2
                            p.op("vector",
                                 lambda e, j=j, jr=jr: e.tensor_mul(
                                     x2st[:, jr, :], hbuf[:, j, :], lrepA[:]),
                                 reads=[B("hbuf"), B("lnrepA"), B(f"x2st{jr}")],
                                 writes=[B(f"x2w{jr}")], tag="nrm1")
                            p.op("vector",
                                 lambda e, jr=jr: e.tensor_add(
                                     x2st[:, jr, :], x2st[:, jr, :], lrepB[:]),
                                 reads=[B(f"x2w{jr}"), B("lnrepB")],
                                 writes=[B(f"x2w{jr}")], tag="nrm2")
                            p.op("scalar",
                                 lambda e, jr=jr: e.activation(
                                     x1tst[:, jr, :], x2st[:, jr, :], AF.Copy,
                                     scale=0.5),
                                 reads=[B(f"x2w{jr}")], writes=[B(f"x1tw{jr}")],
                                 tag="x1cv")
                            p.dma("scalar", f"c_spx{jr}",
                                  lambda e, j=j, jr=jr, csl=csl, b=b: e.dma_start(
                                      out=x1_s[b, j][:, csl], in_=x2st[:, jr, :]),
                                  reads=[B(f"x2w{jr}")],
                                  writes=[B(f"x2st{jr}"), B(f"x1spill_{b}_{j}_{c}")])
                            p.dma("scalar", f"c_spt{jr}",
                                  lambda e, j=j, jr=jr, c=c, b=b: e.dma_start(
                                      out=x1t_b[j][:, b * S + c * 512:
                                                   b * S + (c + 1) * 512],
                                      in_=x1tst[:, jr, :]),
                                  reads=[B(f"x1tw{jr}")],
                                  writes=[B(f"x1tspill_{b}_{j}_{c}")])
                    p.barrier(f"cb{b}")

        with ExitStack() as ed:
            x1c = ed.enter_context(nc.sbuf_tensor("d_x1c", [128, NCK, 512], BF16))
            h1 = ed.enter_context(nc.sbuf_tensor("d_h1", [128, 64, 512], BF16))
            w1blk = ed.enter_context(nc.sbuf_tensor("d_w1", [128, 2, NCK, 128], BF16))
            w2blk = ed.enter_context(nc.sbuf_tensor("d_w2", [128, 2, 2, 32, 128], BF16))
            fbuf = ed.enter_context(nc.sbuf_tensor("d_f", [128, NCK, 512], F32))

            fcv = chcv
            x2ld = cxcf
            wrep = ed.enter_context(nc.sbuf_tensor("d_wrep", [128, 512], F32))
            drepA = lrepA
            drepB = lrepB
            outst = cx2st
            psH = [ed.enter_context(nc.psum_tensor(f"d_psH{r}", [128, 512], F32))
                   for r in range(2)]
            psY2 = [ed.enter_context(nc.psum_tensor(f"d_psY{r}", [128, 512], F32))
                    for r in range(2)]
            psSt2 = [ed.enter_context(nc.psum_tensor(f"d_psS{r}", [128, 512], F32))
                     for r in range(2)]
            psRp2 = [ed.enter_context(nc.psum_tensor(f"d_psR{r}", [128, 512], F32))
                     for r in range(2)]

            NQ = (TOK // 512) if "D" in PH else 0

            def d_loads(q):
                for ck in range(NCK):
                    p.dma("sync", "d_x1",
                          lambda e, ck=ck, q=q: e.dma_start(
                              out=x1c[:, ck, :],
                              in_=x1t_b[ck][:, q * 512:(q + 1) * 512]),
                          reads=[B(f"x1tspill_{q // 4}_{ck}_{q % 4}"), B("x1c")],
                          writes=[B("x1cw")], tag="dx1")

            def w1_load(q, jf):
                rr = w1_b[:, jf * 128:(jf + 1) * 128].rearrange(
                    "(c p) n -> p c n", p=128)
                p.dma("sync", f"d_w1{jf % 2}",
                      lambda e, rr=rr, jf=jf: e.dma_start(
                          out=w1blk[:, jf % 2, :, :], in_=rr),
                      reads=[B("w1_b"), B(f"w1blk{jf % 2}")],
                      writes=[B(f"w1w{jf % 2}")])

            def w2_load(q, j2):
                for hlf in range(2):
                    rr = w2_b[hlf * 4096:(hlf + 1) * 4096,
                              j2 * 128:(j2 + 1) * 128].rearrange(
                        "(c p) n -> p c n", p=128)
                    p.dma("sync", f"d_w2{j2 % 2}",
                          lambda e, rr=rr, j2=j2, hlf=hlf: e.dma_start(
                              out=w2blk[:, j2 % 2, hlf, :, :], in_=rr),
                          reads=[B("w2_b"), B(f"w2blk{j2 % 2}")],
                          writes=[B(f"w2w{j2 % 2}")])

            def x2_load(q, j2):
                bq, cq = q // 4, q % 4
                p.dma("sync", f"d_x2{j2 % 2}",
                      lambda e, j2=j2, bq=bq, cq=cq: e.dma_start(
                          out=x2ld[:, j2 % 2, :],
                          in_=x1_s[bq, j2][:, cq * 512:(cq + 1) * 512]),
                      reads=[B(f"x1spill_{bq}_{j2}_{cq}"), B(f"x2ld{j2 % 2}")],
                      writes=[B(f"x2w{j2 % 2}")])

            for q in range(NQ):
                d_loads(q)
                for ck in range(NCK):
                    p.op("tensor",
                         lambda e, ck=ck: e.matmul(
                             psSt2[0][0:1, :], s_rw[:, ck:ck + 1], x1c[:, ck, :],
                             start=(ck == 0), stop=(ck == NCK - 1)),
                         reads=[B("rw"), B("x1cw")], writes=[B("psSt20")],
                         tag="rt")
                p.op("scalar",
                     lambda e: e.activation(rows[:, 5, :], psSt2[0][0:1, :],
                                            AF.Tanh, bias=s_rbh[:], scale=0.5),
                     reads=[B("psSt20"), B("rbh")], writes=[B("rt_t")])
                p.op("vector",
                     lambda e: e.tensor_scalar(rows[:, 6, :], rows[:, 5, :],
                                               0.5, 0.5, ALU.mult, ALU.add),
                     reads=[B("rt_t")], writes=[B("rt_w_row")])
                rep_row(6, 4, psRp2[0][:], wrep[:], B("wrep"), "rt_w")

                w1_load(q, 0)
                for jf in range(64):
                    hr = jf % 2
                    if jf + 1 < 64:
                        w1_load(q, jf + 1)
                    for ck in range(NCK):
                        p.op("tensor",
                             lambda e, ck=ck, jf=jf, hr=hr: e.matmul(
                                 psH[hr][:], w1blk[:, jf % 2, ck, :],
                                 x1c[:, ck, :],
                                 start=(ck == 0), stop=(ck == NCK - 1)),
                             reads=[B(f"w1w{jf % 2}"), B("x1cw")],
                             writes=[B(f"psH{hr}")], tag="f1mm")
                    p.op("scalar",
                         lambda e, jf=jf, hr=hr: e.activation(
                             h1[:, jf, :], psH[hr][:], AF.Gelu_apprx_tanh,
                             bias=s_b1[:, jf:jf + 1], scale=1.0),
                         reads=[B(f"psH{hr}"), B("b1")],
                         writes=[B(f"h1_{jf}"), B(f"psH{hr}")], tag="gelu")

                def emit_stats_d(j2):
                    p.op("tensor",
                         lambda e, j2=j2: e.matmul(
                             psSt2[0][32:33, :], onesc[:], fcv[:, j2 % 3, 0, :],
                             start=(j2 == 0), stop=(j2 == NCK - 1)),
                         reads=[B("onesc"), B(f"fcv{j2 % 3}")],
                         writes=[B("psSt20")], tag="stm2")
                    p.op("tensor",
                         lambda e, j2=j2: e.matmul(
                             psSt2[1][0:1, :], onesc[:], fcv[:, j2 % 3, 1, :],
                             start=(j2 == 0), stop=(j2 == NCK - 1)),
                         reads=[B("onesc"), B(f"fcq{j2 % 3}")],
                         writes=[B("psSt21")], tag="stm2")

                w2_load(q, 0)
                x2_load(q, 0)
                for j2 in range(NCK):
                    yr = j2 % 2
                    if j2 + 1 < NCK:
                        w2_load(q, j2 + 1)
                        x2_load(q, j2 + 1)
                    for jf in range(64):
                        p.op("tensor",
                             lambda e, jf=jf, j2=j2, yr=yr: e.matmul(
                                 psY2[yr][:],
                                 w2blk[:, j2 % 2, jf // 32, jf % 32, :],
                                 h1[:, jf, :],
                                 start=(jf == 0), stop=(jf == 63)),
                             reads=[B(f"w2w{j2 % 2}"), B(f"h1_{jf}")],
                             writes=[B(f"psY2{yr}")], tag="f2mm")
                    p.op("vector",
                         lambda e, j2=j2, yr=yr: e.scalar_tensor_tensor(
                             fbuf[:, j2, :], psY2[yr][:], s_b2[:, j2:j2 + 1],
                             wrep[:], ALU.add, ALU.mult),
                         reads=[B(f"psY2{yr}"), B("b2"), B("wrep")],
                         writes=[B("fbuf"), B(f"psY2{yr}")], tag="fep1")
                    p.op("vector",
                         lambda e, j2=j2, yr=yr: e.tensor_add(
                             fbuf[:, j2, :], fbuf[:, j2, :], x2ld[:, yr, :]),
                         reads=[B("fbuf"), B(f"x2w{yr}")],
                         writes=[B("fbuf"), B(f"x2ld{yr}")], tag="fep2")
                    p.op("scalar",
                         lambda e, j2=j2: e.copy(fcv[:, j2 % 3, 0, :],
                                                 fbuf[:, j2, :]),
                         reads=[B("fbuf")], writes=[B(f"fcv{j2 % 3}")])
                    p.op("scalar",
                         lambda e, j2=j2: e.activation(fcv[:, j2 % 3, 1, :],
                                                       fbuf[:, j2, :], AF.Square),
                         reads=[B("fbuf")], writes=[B(f"fcq{j2 % 3}")])
                    if j2 >= 2:
                        emit_stats_d(j2 - 2)
                for jl in range(NCK - 2, NCK):
                    emit_stats_d(jl)
                p.op("vector",
                     lambda e: e.tensor_scalar_mul(rows[:, 0, :],
                                                   psSt2[0][32:33, :], 1.0 / D),
                     reads=[B("psSt20")], writes=[B("ln_mu")])
                p.op("vector",
                     lambda e: e.tensor_scalar_mul(rows[:, 1, :],
                                                   psSt2[1][0:1, :], 1.0 / D),
                     reads=[B("psSt21")], writes=[B("ln_msq"), B("psSt21")])
                p.op("vector",
                     lambda e: e.scalar_tensor_tensor(
                         rows[:, 2, :], rows[:, 0, :], 1.0, rows[:, 0, :],
                         ALU.mult, ALU.mult),
                     reads=[B("ln_mu")], writes=[B("ln_mu2")])
                p.op("vector",
                     lambda e: e.tensor_sub(rows[:, 1, :], rows[:, 1, :],
                                            rows[:, 2, :]),
                     reads=[B("ln_msq"), B("ln_mu2")], writes=[B("ln_var")])
                p.op("scalar",
                     lambda e: e.activation(rows[:, 2, :], rows[:, 1, :],
                                            AF.Sqrt, bias=s_eps[:]),
                     reads=[B("ln_var"), B("eps")], writes=[B("ln_std")])
                p.op("vector",
                     lambda e: e.reciprocal(rows[:, 3, :], rows[:, 2, :]),
                     reads=[B("ln_std")], writes=[B("ln_r2_row")])
                p.op("vector",
                     lambda e: e.scalar_tensor_tensor(
                         rows[:, 4, :], rows[:, 0, :], -1.0, rows[:, 3, :],
                         ALU.mult, ALU.mult),
                     reads=[B("ln_mu"), B("ln_r2_row")], writes=[B("ln_n2_row")])
                rep_row(3, 0, psRp2[0][:], drepA[:], B("drepA"), "ln_r2")
                rep_row(4, 2, psRp2[1][:], drepB[:], B("drepB"), "ln_n2")
                bq, cq = q // 4, q % 4
                for j2 in range(NCK):
                    jr = j2 % 2
                    p.op("vector",
                         lambda e, j2=j2, jr=jr: e.tensor_mul(
                             outst[:, jr, :], fbuf[:, j2, :], drepA[:]),
                         reads=[B("fbuf"), B("drepA"), B(f"outst{jr}")],
                         writes=[B(f"outw{jr}")], tag="fin1")
                    p.op("vector",
                         lambda e, jr=jr: e.tensor_add(
                             outst[:, jr, :], outst[:, jr, :], drepB[:]),
                         reads=[B(f"outw{jr}"), B("drepB")],
                         writes=[B(f"outw{jr}")], tag="fin2")
                    p.dma("scalar", f"d_out{jr}",
                          lambda e, j2=j2, jr=jr, bq=bq, cq=cq: e.dma_start(
                              out=yt[bq, j2 * 128:(j2 + 1) * 128,
                                     cq * 512:(cq + 1) * 512],
                              in_=outst[:, jr, :]),
                          reads=[B(f"outw{jr}")],
                          writes=[B(f"outst{jr}"), B("ytout")], tag="ostore")

            for fs in ("d_out0", "d_out1"):
                fin = p.dma_sem(fs)
                p.op("sync", lambda e, fin=fin: e.wait_ge(fin.handle, fin.count))

        p.plan()
        p.emit(nc, es)
    return nc, p



_cached = {}


def _rope_tables():
    half = DH // 2
    inv = 1.0 / (10000.0 ** (np.arange(0, DH, 2, dtype=np.float64) / DH))
    t = np.arange(S, dtype=np.float64)
    fr = np.outer(inv, t)
    cos = np.cos(fr)
    sin = np.sin(fr)
    cosf = np.concatenate([cos, cos], 0)
    sinf = np.concatenate([-sin, sin], 0)
    return (np.ascontiguousarray(cosf.astype(np.float32).astype(BF)),
            np.ascontiguousarray(sinf.astype(np.float32).astype(BF)))


def kernel(L0_attn_x, L0_attn_q_w, L0_attn_k_w, L0_attn_v_w, L0_attn_q_b,
           L0_attn_k_b, L0_attn_v_b, L0_attn_out_w, L0_attn_out_b,
           L0_ffn_router_weight, L0_ffn_router_bias, L0_ffn_w1, L0_ffn_b1,
           L0_ffn_w2, L0_ffn_b2):
    n_cores = 8
    if "nc" not in _cached:
        _cached["nc"] = build_program()
    nc, _ = _cached["nc"]

    cosk_t, sink_t = _rope_tables()

    f32 = np.float32
    wqt = np.ascontiguousarray(np.asarray(L0_attn_q_w, f32).T)
    wkt = np.ascontiguousarray(np.asarray(L0_attn_k_w, f32).T)
    wvt = np.ascontiguousarray(np.asarray(L0_attn_v_w, f32).T)
    wot = np.ascontiguousarray(np.asarray(L0_attn_out_w, f32).T)
    w1 = np.ascontiguousarray(np.asarray(L0_ffn_w1, f32))
    w2 = np.ascontiguousarray(np.asarray(L0_ffn_w2, f32))
    qb2 = np.ascontiguousarray(np.asarray(L0_attn_q_b, f32).reshape(NCK, 128).T)
    kb2 = np.ascontiguousarray(np.asarray(L0_attn_k_b, f32).reshape(NCK, 128).T)
    vbr = np.ascontiguousarray(np.broadcast_to(np.asarray(L0_attn_v_b, f32).reshape(1, D), (128, D)))
    ob2 = np.ascontiguousarray(np.asarray(L0_attn_out_b, f32).reshape(NCK, 128).T)
    b1c = np.ascontiguousarray(np.asarray(L0_ffn_b1, f32).reshape(64, 128).T)
    b2c = np.ascontiguousarray(np.asarray(L0_ffn_b2, f32).reshape(NCK, 128).T)
    rwc = np.ascontiguousarray(
        np.asarray(L0_ffn_router_weight, f32).reshape(NCK, 128).T)
    rbx = np.asarray(L0_ffn_router_bias, f32).reshape(1, 1)
    idf = np.eye(128, dtype=f32)
    onesc = np.ones((128, 1), BF)
    onesr = np.ones((1, 128), BF)
    ones1f = np.ones((1, 128), f32)

    x = np.asarray(L0_attn_x, f32)
    in_maps = []
    for core in range(n_cores):
        xs = x[core * BL:(core + 1) * BL]
        xtin = np.ascontiguousarray(xs.transpose(0, 2, 1))
        in_maps.append(dict(
            xtin=xtin, wqt=wqt, wkt=wkt, wvt=wvt, wot=wot, w1=w1, w2=w2,
            qb2=qb2, kb2=kb2, vbr=vbr, ob2=ob2, b1c=b1c, b2c=b2c, rwc=rwc,
            rbx=rbx, cosk=cosk_t, sink=sink_t,
            idf=idf, onesc=onesc, onesr=onesr, ones1f=ones1f,
        ))

    res = run_bass_kernel_spmd(nc, in_maps, core_ids=list(range(n_cores)))
    out = np.empty((16, S, D), np.float32)
    for core in range(n_cores):
        ytc = res.results[core]["yt"]
        out[core * BL:(core + 1) * BL] = ytc.transpose(0, 2, 1)
    return out


# version 05
# speedup vs baseline: 1.1396x; 1.0860x over previous
import numpy as np
import ml_dtypes
from collections import defaultdict
from contextlib import ExitStack

import os
import concourse.bass as bass
import concourse.mybir as mybir
from concourse.bass_utils import run_bass_kernel_spmd

F32 = mybir.dt.float32
BF16 = mybir.dt.bfloat16
AF = mybir.ActivationFunctionType
ALU = mybir.AluOpType
AX = mybir.AxisListType
BF = ml_dtypes.bfloat16

ENGINES = ("tensor", "vector", "scalar", "gpsimd", "sync")

D = 2048
S = 2048
H = 16
DH = 128
DFF = 8192
NCK = 16
BL = 2
TOK = BL * S



class Buf:
    __slots__ = ("name", "writer", "readers")

    def __init__(self, name):
        self.name = name
        self.writer = None
        self.readers = []


class DmaSem:
    __slots__ = ("name", "count", "handle", "final_wait")

    def __init__(self, name, final_wait=False):
        self.name = name
        self.count = 0
        self.handle = None
        self.final_wait = final_wait


class _Op:
    __slots__ = ("engine", "emit_fn", "reads", "writes", "dma_sem", "dma_cum",
                 "inc_needed", "inc_count", "waits", "tag", "extra_dma_waits",
                 "barrier_state")

    def __init__(self, engine, emit_fn, reads, writes, dma_sem, tag):
        self.engine = engine
        self.emit_fn = emit_fn
        self.reads = list(reads)
        self.writes = list(writes)
        self.dma_sem = dma_sem
        self.dma_cum = 0
        self.inc_needed = False
        self.inc_count = 0
        self.waits = []
        self.tag = tag
        self.extra_dma_waits = None
        self.barrier_state = None


class Prog:
    def __init__(self):
        self.ops = []
        self.dma_sems = {}
        self._bufs = {}
        self._last_op = {}

    def buf(self, name):
        b = self._bufs.get(name)
        if b is None:
            b = self._bufs[name] = Buf(name)
        return b

    def dma_sem(self, name):
        s = self.dma_sems.get(name)
        if s is None:
            s = self.dma_sems[name] = DmaSem(
                name, final_wait=name.startswith(("cst", "prep")))
        return s

    def op(self, engine, emit_fn, reads=(), writes=(), dma_sem=None, tag=""):
        o = _Op(engine, emit_fn, reads, writes, dma_sem, tag)
        if dma_sem is not None:
            dma_sem.count += 16
            o.dma_cum = dma_sem.count
        self.ops.append(o)
        self._last_op[engine] = o
        return o

    def dma(self, engine, sem_name, emit_fn, reads=(), writes=(), tag=""):
        return self.op(engine, emit_fn, reads, writes, self.dma_sem(sem_name), tag)

    def barrier(self, tag="bar"):
        o = _Op("sync", None, (), (), None, f"{tag}S")
        o.barrier_state = (dict(self._last_op),
                           [(sm, sm.count) for sm in self.dma_sems.values()])
        self.ops.append(o)

    def plan(self):
        for b in self._bufs.values():
            b.writer = None
            b.readers = []
        deps = [set() for _ in self.ops]
        idx = {}
        for i, o in enumerate(self.ops):
            idx[id(o)] = i
        for i, o in enumerate(self.ops):
            if o.barrier_state is not None:
                for b in self._bufs.values():
                    b.writer = None
                    b.readers = []
                continue
            for b in o.reads:
                w = b.writer
                if w is not None and w is not o:
                    deps[i].add(w)
            for b in o.writes:
                w = b.writer
                if w is not None and w is not o:
                    same_group = (o.dma_sem is not None
                                  and w.dma_sem is o.dma_sem and not b.readers)
                    if not same_group:
                        deps[i].add(w)
                for r in b.readers:
                    if r is not o:
                        deps[i].add(r)
            for b in o.reads:
                b.readers.append(o)
            for b in o.writes:
                b.writer = o
                b.readers = []

        eng_ops = defaultdict(list)
        for o in self.ops:
            if o.barrier_state is None:
                eng_ops[o.engine].append(o)

        op_knowledge = {}
        eng_know = {e: {} for e in ENGINES}
        wait_edges = [[] for _ in self.ops]
        cur_barrier = None
        for i, o in enumerate(self.ops):
            if o.barrier_state is not None:
                cur_barrier = o.barrier_state
                continue
            know = eng_know[o.engine]
            best = {}
            dd = set(deps[i])
            if cur_barrier is not None:
                last_ops, _ = cur_barrier
                for e2, w2 in last_ops.items():
                    if w2 is not None:
                        dd.add(w2)
            for w in sorted(dd, key=lambda x: idx[id(x)]):
                if w.engine == "tensor" and o.engine == "tensor" and w.dma_sem is None:
                    continue
                if w.dma_sem is not None:
                    key = ("dma", w.dma_sem.name)
                    val = w.dma_sem.count if w.dma_sem.final_wait else w.dma_cum
                    cur = know.get(key)
                    if cur is not None and cur >= val:
                        continue
                    if key not in best or best[key][0] < val:
                        best[key] = (val, w)
                    know[key] = val
                else:
                    key = ("eng", w.engine)
                    wpos = idx[id(w)]
                    cur = know.get(key)
                    if cur is not None and cur >= wpos:
                        continue
                    if key not in best or best[key][0] < wpos:
                        best[key] = (wpos, w)
                    know[key] = wpos
                pk = op_knowledge.get(id(w))
                if pk:
                    for k2, v2 in pk.items():
                        if know.get(k2, -1) < v2:
                            know[k2] = v2
            if cur_barrier is not None:
                for sem, cnt in cur_barrier[1]:
                    key = ("dma", sem.name)
                    if know.get(key, -1) < cnt:
                        if key not in best or best[key][0] < cnt:
                            best[key] = (cnt, None)
                        know[key] = cnt
            if o.extra_dma_waits:
                for sem, cnt in o.extra_dma_waits:
                    key = ("dma", sem.name)
                    if know.get(key, -1) < cnt:
                        if key not in best or best[key][0] < cnt:
                            best[key] = (cnt, None)
                        know[key] = cnt
            wait_edges[i] = [(k, v, w) for k, (v, w) in best.items()]
            op_knowledge[id(o)] = dict(know)

        for i, o in enumerate(self.ops):
            for key, val, w in wait_edges[i]:
                if key[0] == "eng":
                    w.inc_needed = True
        inc_counts = defaultdict(int)
        for o in self.ops:
            if o.dma_sem is None and o.inc_needed:
                inc_counts[o.engine] += 1
                o.inc_count = inc_counts[o.engine]
        for i, o in enumerate(self.ops):
            best = {}
            for key, val, w in wait_edges[i]:
                v = w.inc_count if (key[0] == "eng") else val
                if best.get(key, -1) < v:
                    best[key] = v
            o.waits = sorted(best.items(), key=lambda kv: (kv[0][0], str(kv[0][1])))
        self._eng_ops = eng_ops
        return self

    def emit(self, nc, es):
        eng_sems = {}
        for e in ENGINES:
            if any(o.inc_needed for o in self._eng_ops.get(e, ())):
                eng_sems[e] = es.enter_context(nc.semaphore(f"es_{e}"))
        for name, dssem in self.dma_sems.items():
            dssem.handle = es.enter_context(nc.semaphore(f"ds_{name}"))

        def sem_handle(key):
            return eng_sems[key[1]] if key[0] == "eng" else self.dma_sems[key[1]].handle

        block = es.enter_context(nc.Block())

        def make_body(e):
            def body(eng):
                for o in self._eng_ops.get(e, ()):
                    for key, v in o.waits:
                        eng.wait_ge(sem_handle(key), v)
                    ins = o.emit_fn(eng)
                    if o.dma_sem is not None:
                        ins.then_inc(o.dma_sem.handle, 16)
                    elif o.inc_needed:
                        ins.then_inc(eng_sems[o.engine], 1)
            return body

        for e in ENGINES:
            if self._eng_ops.get(e):
                getattr(block, e)(make_body(e))

    def stats(self):
        from collections import Counter
        c = Counter(o.engine for o in self.ops)
        return (f"ops={dict(c)} total={len(self.ops)} "
                f"waits={sum(len(o.waits) for o in self.ops)} "
                f"incs={sum(1 for o in self.ops if o.inc_needed)} "
                f"dma_sems={len(self.dma_sems)}")



def build_program():
    PH = os.environ.get("MK_PHASES", "ABCD")
    PREP = os.environ.get("MK_PREP", "all")
    ABITS = os.environ.get("MK_ABITS", "full")
    AJ = int(os.environ.get("MK_AJ", "16"))
    ABATCH = int(os.environ.get("MK_ABATCH", str(BL)))
    nc = bass.Bass("TRN2", target_bir_lowering=False)

    xtin = nc.dram_tensor("xtin", [BL, D, S], F32, kind="ExternalInput").ap()
    wqt = nc.dram_tensor("wqt", [D, D], F32, kind="ExternalInput").ap()
    wkt = nc.dram_tensor("wkt", [D, D], F32, kind="ExternalInput").ap()
    wvt = nc.dram_tensor("wvt", [D, D], F32, kind="ExternalInput").ap()
    wot = nc.dram_tensor("wot", [D, D], F32, kind="ExternalInput").ap()
    w1 = nc.dram_tensor("w1", [D, DFF], F32, kind="ExternalInput").ap()
    w2 = nc.dram_tensor("w2", [DFF, D], F32, kind="ExternalInput").ap()
    qb2 = nc.dram_tensor("qb2", [128, NCK], F32, kind="ExternalInput").ap()
    kb2 = nc.dram_tensor("kb2", [128, NCK], F32, kind="ExternalInput").ap()
    vbr = nc.dram_tensor("vbr", [128, D], F32, kind="ExternalInput").ap()
    ob2 = nc.dram_tensor("ob2", [128, NCK], F32, kind="ExternalInput").ap()
    b1c = nc.dram_tensor("b1c", [128, 64], F32, kind="ExternalInput").ap()
    b2c = nc.dram_tensor("b2c", [128, NCK], F32, kind="ExternalInput").ap()
    rwc = nc.dram_tensor("rwc", [128, NCK], F32, kind="ExternalInput").ap()
    rbx = nc.dram_tensor("rbx", [1, 1], F32, kind="ExternalInput").ap()
    cosk = nc.dram_tensor("cosk", [128, S], BF16, kind="ExternalInput").ap()
    sink = nc.dram_tensor("sink", [128, S], BF16, kind="ExternalInput").ap()
    idfh = nc.dram_tensor("idf", [128, 128], F32, kind="ExternalInput").ap()
    onesch = nc.dram_tensor("onesc", [128, 1], BF16, kind="ExternalInput").ap()
    onesrh = nc.dram_tensor("onesr", [1, 128], BF16, kind="ExternalInput").ap()
    ones1fh = nc.dram_tensor("ones1f", [1, 128], F32, kind="ExternalInput").ap()

    yt = nc.dram_tensor("yt", [BL, D, S], F32, kind="ExternalOutput").ap()

    wqt_b = nc.dram_tensor("wqt_b", [D, D], BF16, kind="Internal").ap()
    wkt_b = nc.dram_tensor("wkt_b", [D, D], BF16, kind="Internal").ap()
    wvt_b = nc.dram_tensor("wvt_b", [D, D], BF16, kind="Internal").ap()
    wot_b = nc.dram_tensor("wot_b", [D, D], BF16, kind="Internal").ap()
    w1_b = nc.dram_tensor("w1_b", [D, DFF], BF16, kind="Internal").ap()
    w2_b = nc.dram_tensor("w2_b", [DFF, D], BF16, kind="Internal").ap()
    qt_s = nc.dram_tensor("qt_s", [BL, H, 128, S], BF16, kind="Internal").ap()
    kt_s = nc.dram_tensor("kt_s", [BL, H, 128, S], BF16, kind="Internal").ap()
    v_s = nc.dram_tensor("v_s", [BL, H, 16, 128, 128], BF16, kind="Internal").ap()
    x1_s = nc.dram_tensor("x1_s", [BL, NCK, 128, S], F32, kind="Internal").ap()
    x1t_b = nc.dram_tensor("x1t_b", [NCK, 128, TOK], BF16, kind="Internal").ap()

    p = Prog()
    B = p.buf
    es = ExitStack()
    with es:
        idf = nc.alloc_sbuf_tensor("c_idf", [128, 128], F32)
        onesc = nc.alloc_sbuf_tensor("c_onesc", [128, 1], BF16)
        onesr = nc.alloc_sbuf_tensor("c_onesr", [1, 128], BF16)
        ones1f = nc.alloc_sbuf_tensor("c_ones1f", [1, 128], F32)
        s_qb = nc.alloc_sbuf_tensor("c_qb", [128, NCK], F32)
        s_kb = nc.alloc_sbuf_tensor("c_kb", [128, NCK], F32)
        s_ob = nc.alloc_sbuf_tensor("c_ob", [128, NCK], F32)
        s_b1 = nc.alloc_sbuf_tensor("c_b1", [128, 64], F32)
        s_b2 = nc.alloc_sbuf_tensor("c_b2", [128, NCK], F32)
        s_rwf = nc.alloc_sbuf_tensor("c_rwf", [128, NCK], F32)
        s_rw = nc.alloc_sbuf_tensor("c_rw", [128, NCK], BF16)
        s_rb = nc.alloc_sbuf_tensor("c_rb", [1, 1], F32)
        s_rbh = nc.alloc_sbuf_tensor("c_rbh", [1, 1], F32)
        s_qbs = nc.alloc_sbuf_tensor("c_qbs", [128, NCK], F32)
        negshift = nc.alloc_sbuf_tensor("c_negsh", [128, 2], F32)
        maxcol = nc.alloc_sbuf_tensor("c_maxcol", [128, 1], F32)
        negsh1 = nc.alloc_sbuf_tensor("c_negsh1", [1, 1], F32)
        lrepA = nc.alloc_sbuf_tensor("c_lrepA", [128, 512], F32)
        lrepB = nc.alloc_sbuf_tensor("c_lrepB", [128, 512], F32)
        cx2st = nc.alloc_sbuf_tensor("c_x2st", [128, 2, 512], F32)
        cx1tst = nc.alloc_sbuf_tensor("c_x1tst", [128, 2, 512], BF16)
        chcv = nc.alloc_sbuf_tensor("c_hcv", [128, 3, 2, 512], BF16)
        cxcf = nc.alloc_sbuf_tensor("c_xcf", [128, 2, 512], F32)
        rows = nc.alloc_sbuf_tensor("c_rows", [1, 7, 512], F32)
        rowsb = nc.alloc_sbuf_tensor("c_rowsb", [1, 6, 512], BF16)
        s_eps = nc.alloc_sbuf_tensor("c_eps", [1, 1], F32)

        for ap_h, t, nm in ((idfh, idf, "idf"), (onesch, onesc, "onesc"),
                            (onesrh, onesr, "onesr"), (ones1fh, ones1f, "ones1f"),
                            (qb2, s_qb, "qb"), (kb2, s_kb, "kb"),
                            (ob2, s_ob, "ob"), (b1c, s_b1, "b1"),
                            (b2c, s_b2, "b2"), (rbx, s_rb, "rb")):
            p.dma("sync", "cst", lambda e, a=ap_h, t=t: e.dma_start(out=t[:], in_=a),
                  writes=[B(nm)])
        p.dma("sync", "cst",
              lambda e: e.dma_start(out=s_rwf[:], in_=rwc), writes=[B("rwf")])
        p.op("vector", lambda e: e.tensor_copy(s_rw[:], s_rwf[:]),
             reads=[B("rwf")], writes=[B("rw")])
        p.op("scalar", lambda e: e.mul(s_rbh[:], s_rb[:], 0.5),
             reads=[B("rb")], writes=[B("rbh")])
        p.op("scalar", lambda e: e.mul(s_qbs[:], s_qb[:], DH ** -0.5),
             reads=[B("qb")], writes=[B("qbs")])
        p.op("vector", lambda e: e.memset(negshift[:], 0.0),
             writes=[B("sh0"), B("sh1")])
        p.op("vector", lambda e: e.memset(s_eps[:], 1e-5), writes=[B("eps")])

        def cast_w(src, dst, rows_total, cols, nm, chunk=128):
            for r0 in range(0, rows_total, chunk):
                p.dma("gpsimd", f"prep_{nm}",
                      lambda e, s=src[r0:r0 + chunk, :], d=dst[r0:r0 + chunk, :]:
                      e.dma_start(out=d, in_=s),
                      writes=[B(nm)], tag=f"cast_{nm}")
        if PREP in ("all", "casts"):
            cast_w(wqt, wqt_b, D, D, "wqt_b")
            cast_w(wkt, wkt_b, D, D, "wkt_b")
            cast_w(wvt, wvt_b, D, D, "wvt_b")
            cast_w(wot, wot_b, D, D, "wot_b")
            cast_w(w1, w1_b, D, DFF, "w1_b")
            cast_w(w2, w2_b, DFF, D, "w2_b")

        if PREP == "bar":
            p.barrier("pvb0")

        def rep_row(slot, sb, ps, dest, dest_buf, pref):
            hi_b = B(f"{pref}_hi")
            lo_b = B(f"{pref}_lo")
            p.op("scalar",
                 lambda e: e.copy(rowsb[:, sb, :], rows[:, slot, :]),
                 reads=[B(f"{pref}_row")], writes=[hi_b])
            p.op("vector",
                 lambda e: e.tensor_sub(rowsb[:, sb + 1, :], rows[:, slot, :],
                                        rowsb[:, sb, :]),
                 reads=[B(f"{pref}_row"), hi_b], writes=[lo_b])
            p.op("tensor",
                 lambda e: e.matmul(ps, onesr[:], rowsb[:, sb, :],
                                    start=True, stop=False),
                 reads=[B("onesr"), hi_b], writes=[B(f"{pref}_ps")])
            p.op("tensor",
                 lambda e: e.matmul(ps, onesr[:], rowsb[:, sb + 1, :],
                                    start=False, stop=True),
                 reads=[B("onesr"), lo_b], writes=[B(f"{pref}_ps")])
            p.op("vector",
                 lambda e: e.tensor_copy(dest, ps),
                 reads=[B(f"{pref}_ps")], writes=[dest_buf])

        for b in range(min(BL, ABATCH) if "A" in PH else 0):
            with ExitStack() as ea:
                XT = ea.enter_context(nc.sbuf_tensor(f"a_xt{b}", [128, NCK, S], BF16))
                vbrep = ea.enter_context(nc.sbuf_tensor(f"a_vbrep{b}", [128, D], F32))
                p.dma("sync", f"cstA{b}", lambda e: e.dma_start(out=vbrep[:], in_=vbr),
                      writes=[B("vbrep")])
                t_cosk = ea.enter_context(nc.sbuf_tensor(f"a_cosk{b}", [128, S], BF16))
                t_sink = ea.enter_context(nc.sbuf_tensor(f"a_sink{b}", [128, S], BF16))
                for ap_h, t, nm in ((cosk, t_cosk, "cosk"), (sink, t_sink, "sink")):
                    p.dma("sync", f"cstA{b}",
                          lambda e, a=ap_h, t=t: e.dma_start(out=t[:], in_=a),
                          writes=[B(nm)])
                xtf = ea.enter_context(nc.sbuf_tensor(f"a_xtf{b}", [128, 2, 1024], F32))
                wqk = ea.enter_context(nc.sbuf_tensor(f"a_wqk{b}", [128, 2, 2, NCK, 128], BF16))
                wvb = ea.enter_context(nc.sbuf_tensor(f"a_wv{b}", [128, 2, NCK, 512], BF16))
                abuf = ea.enter_context(nc.sbuf_tensor(f"a_ab{b}", [128, 3, 512], BF16))
                asw = ea.enter_context(nc.sbuf_tensor(f"a_as{b}", [128, 3, 512], BF16))
                tt1 = ea.enter_context(nc.sbuf_tensor(f"a_t1{b}", [128, 2, 512], BF16))
                tt2 = ea.enter_context(nc.sbuf_tensor(f"a_t2{b}", [128, 2, 512], BF16))
                qst = ea.enter_context(nc.sbuf_tensor(f"a_qst{b}", [128, 2, S], BF16))
                vst = ea.enter_context(nc.sbuf_tensor(f"a_vst{b}", [128, 2, 512], BF16))
                psA = [ea.enter_context(nc.psum_tensor(f"a_psA{b}{r}", [128, 512], F32))
                       for r in range(2)]
                psV = [ea.enter_context(nc.psum_tensor(f"a_psV{b}{r}", [128, 512], F32))
                       for r in range(2)]

                for ch in range(NCK * 2):
                    c, hf = ch // 2, ch % 2
                    p.dma("sync", f"a_x{ch % 2}",
                          lambda e, c=c, hf=hf, ch=ch, b=b: e.dma_start(
                              out=xtf[:, ch % 2, :],
                              in_=xtin[b, c * 128:(c + 1) * 128,
                                       hf * 1024:(hf + 1) * 1024]),
                          reads=[B(f"xtf{ch % 2}")], writes=[B(f"xtfw{ch % 2}")])
                    p.op("scalar",
                         lambda e, c=c, hf=hf, ch=ch: e.copy(
                             XT[:, c, hf * 1024:(hf + 1) * 1024], xtf[:, ch % 2, :]),
                         reads=[B(f"xtfw{ch % 2}")],
                         writes=[B("XT"), B(f"xtf{ch % 2}")], tag="xtcv")

                def wq_load(w, j):
                    src = (wqt_b if w == 0 else wkt_b)
                    nm = ("wqt_b" if w == 0 else "wkt_b")
                    rr = src[:, j * 128:(j + 1) * 128].rearrange(
                        "(c p) n -> p c n", p=128)
                    p.dma("sync", f"a_w{w}{j % 2}",
                          lambda e, rr=rr, w=w, j=j: e.dma_start(
                              out=wqk[:, w, j % 2, :, :], in_=rr),
                          reads=[B(nm), B(f"wqk{w}{j % 2}")],
                          writes=[B(f"wqkw{w}{j % 2}")])

                def qk_block(w, j):
                    tabc = t_cosk
                    tabs = t_sink
                    bias = s_qbs if w == 0 else s_kb
                    scl = DH ** -0.5 if w == 0 else 1.0
                    wb = B(f"wqkw{w}{j % 2}")
                    for c in range(4):
                        sr = (j * 4 + c) % 2
                        ar = (j * 4 + c) % 3
                        sl = slice(c * 512, (c + 1) * 512)
                        for ck in range(NCK):
                            p.op("tensor",
                                 lambda e, w=w, j=j, ck=ck, sl=sl, sr=sr: e.matmul(
                                     psA[sr][:], wqk[:, w, j % 2, ck, :],
                                     XT[:, ck, sl],
                                     start=(ck == 0), stop=(ck == NCK - 1)),
                                 reads=[wb, B("XT")],
                                 writes=[B(f"psA{sr}")], tag="qkmm")
                        if ABITS == "xqm":
                            continue
                        if ABITS == "xqc":
                            p.op("scalar",
                                 lambda e, sr=sr, ar=ar: e.copy(
                                     abuf[:, ar, :], psA[sr][:]),
                                 reads=[B(f"psA{sr}")],
                                 writes=[B(f"ab{ar}"), B(f"psA{sr}")], tag="qkev")
                        else:
                            p.op("scalar",
                                 lambda e, j=j, sr=sr, ar=ar, bias=bias, scl=scl: e.activation(
                                     abuf[:, ar, :], psA[sr][:], AF.Identity,
                                     bias=bias[:, j:j + 1], scale=scl),
                                 reads=[B(f"psA{sr}"), B("qbs" if w == 0 else "kb")],
                                 writes=[B(f"ab{ar}"), B(f"psA{sr}")], tag="qkev")
                        if ABITS in ("xq", "xqm", "xqc"):
                            continue
                        p.dma("scalar", f"a_sw{ar}",
                              lambda e, ar=ar: e.dma_start(out=asw[0:64, ar, :],
                                                           in_=abuf[64:128, ar, :]),
                              reads=[B(f"ab{ar}")], writes=[B(f"as{ar}")])
                        p.dma("scalar", f"a_sw{ar}",
                              lambda e, ar=ar: e.dma_start(out=asw[64:128, ar, :],
                                                           in_=abuf[0:64, ar, :]),
                              reads=[B(f"ab{ar}")], writes=[B(f"as{ar}")])
                        tr = (j * 4 + c) % 2
                        tcn = "cosk"
                        tsn = "sink"
                        p.op("vector",
                             lambda e, ar=ar, tr=tr, sl=sl, tabc=tabc: e.tensor_mul(
                                 tt1[:, tr, :], abuf[:, ar, :], tabc[:, sl]),
                             reads=[B(f"ab{ar}"), B(tcn)],
                             writes=[B(f"tt1{tr}")], tag="rope1")
                        p.op("vector",
                             lambda e, ar=ar, tr=tr, sl=sl, tabs=tabs: e.tensor_mul(
                                 tt2[:, tr, :], asw[:, ar, :], tabs[:, sl]),
                             reads=[B(f"as{ar}"), B(tsn)],
                             writes=[B(f"tt2{tr}"), B(f"ab{ar}")], tag="rope2")
                        p.op("vector",
                             lambda e, tr=tr, sl=sl, j=j: e.tensor_add(
                                 qst[:, j % 2, sl], tt1[:, tr, :], tt2[:, tr, :]),
                             reads=[B(f"tt1{tr}"), B(f"tt2{tr}")],
                             writes=[B(f"qst{j % 2}"), B(f"as{(j * 4 + c) % 3}")],
                             tag="rope3")
                    if ABITS in ("xq", "xqm", "xqc", "xqr"):
                        return
                    dst = (qt_s if w == 0 else kt_s)[b, j]
                    p.dma("scalar", f"a_sp{w}{j % 2}",
                          lambda e, dst=dst, j=j: e.dma_start(out=dst,
                                                              in_=qst[:, j % 2, :]),
                          reads=[B(f"qst{j % 2}")], writes=[B(f"qkspill{w}_{j}")])

                nj = min(H, AJ) if ABITS != "x" else 0
                for w in (0, 1):
                    wq_load(w, 0)
                    for j in range(nj):
                        if j + 1 < nj:
                            wq_load(w, j + 1)
                        qk_block(w, j)

                def wv_load(c4):
                    rr = wvt_b[:, c4 * 512:(c4 + 1) * 512].rearrange(
                        "(c p) n -> p c n", p=128)
                    p.dma("sync", f"a_wv{c4 % 2}",
                          lambda e, rr=rr, c4=c4: e.dma_start(
                              out=wvb[:, c4 % 2, :, :], in_=rr),
                          reads=[B("wvt_b"), B(f"wvb{c4 % 2}")],
                          writes=[B(f"wvbw{c4 % 2}")])

                wv_load(0)
                for c4 in range(4 if ABITS == "full" else 0):
                    if c4 + 1 < 4:
                        wv_load(c4 + 1)
                    dsl = slice(c4 * 512, (c4 + 1) * 512)
                    for t in range(16):
                        vr = (c4 * 16 + t) % 2
                        for ck in range(NCK):
                            p.op("tensor",
                                 lambda e, ck=ck, t=t, c4=c4, vr=vr: e.matmul(
                                     psV[vr][:], XT[:, ck, t * 128:(t + 1) * 128],
                                     wvb[:, c4 % 2, ck, :],
                                     start=(ck == 0), stop=(ck == NCK - 1)),
                                 reads=[B(f"wvbw{c4 % 2}"), B("XT")],
                                 writes=[B(f"psV{vr}")], tag="vmm")
                        p.op("vector",
                             lambda e, vr=vr, dsl=dsl: e.scalar_tensor_tensor(
                                 vst[:, vr, :], psV[vr][:], 1.0, vbrep[:, dsl],
                                 ALU.mult, ALU.add),
                             reads=[B(f"psV{vr}"), B("vbrep")],
                             writes=[B(f"vst{vr}"), B(f"psV{vr}")], tag="vev")
                        for hh in range(4):
                            h = c4 * 4 + hh
                            p.dma("scalar", f"a_spv{vr}",
                                  lambda e, h=h, t=t, vr=vr, hh=hh, b=b: e.dma_start(
                                      out=v_s[b, h, t],
                                      in_=vst[:, vr, hh * 128:(hh + 1) * 128]),
                                  reads=[B(f"vst{vr}")], writes=[B(f"vspill{h}_{t}")])
                p.barrier(f"ab{b}")

            if "B" not in PH:
                continue
            with ExitStack() as eo:
                OT = eo.enter_context(nc.sbuf_tensor(f"b_ot{b}", [128, NCK, S], BF16))

                with ExitStack() as eb:
                    qtb = eb.enter_context(nc.sbuf_tensor(f"b_qt{b}", [128, 2, S], BF16))
                    ktb = eb.enter_context(nc.sbuf_tensor(f"b_kt{b}", [128, 2, S], BF16))
                    vtb = eb.enter_context(nc.sbuf_tensor(f"b_vt{b}", [128, 2, 16, 128], BF16))
                    pt = eb.enter_context(nc.sbuf_tensor(f"b_pt{b}", [128, 5, 1024], BF16))
                    lrec = eb.enter_context(nc.sbuf_tensor(f"b_rc{b}", [1, 2, 512], F32))
                    lrhi = eb.enter_context(nc.sbuf_tensor(f"b_rh{b}", [1, 2, 512], BF16))
                    lrlo = eb.enter_context(nc.sbuf_tensor(f"b_rl{b}", [1, 2, 512], BF16))
                    reps = cxcf
                    psS = [eb.enter_context(
                        nc.psum_tensor(f"b_psS{b}{r}", [128, 1024], F32))
                        for r in range(2)]
                    psO = [eb.enter_context(
                        nc.psum_tensor(f"b_psO{b}{r}", [128, 512], F32))
                        for r in range(2)]
                    psR = [eb.enter_context(
                        nc.psum_tensor(f"b_psR{b}{r}", [128, 512], F32))
                        for r in range(2)]
                    psD = [psR[r][0:1, :] for r in range(2)]
                    psT = psS[0][0:1, 0:128]
                    psB_ = psS[0][0:128, 512 + 256:512 + 257]

                    def head_load(h):
                        hr = h % 2
                        p.dma("sync", f"b_hq{hr}",
                              lambda e, h=h, hr=hr, b=b: e.dma_start(
                                  out=qtb[:, hr, :], in_=qt_s[b, h]),
                              reads=[B(f"qkspill0_{h}"), B(f"qtb{hr}")],
                              writes=[B(f"qtbw{hr}")])
                        p.dma("sync", f"b_hk{hr}",
                              lambda e, h=h, hr=hr, b=b: e.dma_start(
                                  out=ktb[:, hr, :], in_=kt_s[b, h]),
                              reads=[B(f"qkspill1_{h}"), B(f"ktb{hr}")],
                              writes=[B(f"ktbw{hr}")])
                        rr = v_s[b, h].rearrange("t p d -> p t d")
                        p.dma("sync", f"b_hv{hr}",
                              lambda e, rr=rr, hr=hr: e.dma_start(
                                  out=vtb[:, hr, :, :], in_=rr),
                              reads=[B(f"vtb{hr}")] + [B(f"vspill{h}_{t}") for t in range(16)],
                              writes=[B(f"vtbw{hr}")])

                    def b_scores(h, c, g):
                        hr = h % 2
                        sr = g % 2
                        for hf in range(2):
                            j = 2 * g + hf
                            p.op("tensor",
                                 lambda e, j=j, sr=sr, hf=hf, c=c, hr=hr: e.matmul(
                                     psS[sr][:, hf * 512:(hf + 1) * 512],
                                     ktb[:, hr, j * 128:(j + 1) * 128],
                                     qtb[:, hr, c * 512:(c + 1) * 512],
                                     start=True, stop=True),
                                 reads=[B(f"ktbw{hr}"), B(f"qtbw{hr}")],
                                 writes=[B(f"psS{sr}")], tag="sc")

                    def b_shift_chain(h):
                        w = (h + 1) % 2
                        p.op("vector",
                             lambda e: e.reduce_max(maxcol[:], psS[0][:, 0:512],
                                                    AX.X, negate=True),
                             reads=[B("psS0")], writes=[B("maxcol")])
                        p.op("tensor", lambda e: e.transpose(psT, maxcol[:], idf[:]),
                             reads=[B("maxcol"), B("idf")], writes=[B("psS0")])
                        p.op("vector",
                             lambda e: e.tensor_reduce(negsh1[:], psT, AX.X, ALU.min),
                             reads=[B("psS0")], writes=[B("negsh1")])
                        p.op("tensor",
                             lambda e: e.matmul(psB_, ones1f[:], negsh1[:],
                                                start=True, stop=True),
                             reads=[B("negsh1"), B("ones1f")], writes=[B("psS0")])
                        p.op("scalar",
                             lambda e, w=w: e.copy(negshift[:, w:w + 1], psB_),
                             reads=[B("psS0")], writes=[B(f"sh{w}")])

                    def b_pv_dsum(h, c, g):
                        hr = h % 2
                        oc = c % 2
                        pr = g % 5
                        for hf in range(2):
                            j = 2 * g + hf
                            p.op("tensor",
                                 lambda e, j=j, oc=oc, pr=pr, hf=hf, hr=hr: e.matmul(
                                     psO[oc][:], vtb[:, hr, j, :],
                                     pt[:, pr, hf * 512:(hf + 1) * 512],
                                     start=(j == 0), stop=(j == 15)),
                                 reads=[B(f"vtbw{hr}"), B(f"pt{pr}")],
                                 writes=[B(f"psO{oc}")], tag="pv")
                            p.op("tensor",
                                 lambda e, j=j, m=c % 2, pr=pr, hf=hf: e.matmul(
                                     psD[m], onesc[:],
                                     pt[:, pr, hf * 512:(hf + 1) * 512],
                                     start=(j == 0), stop=(j == 15)),
                                 reads=[B("onesc"), B(f"pt{pr}")],
                                 writes=[B(f"bkDR{c % 2}")], tag="ds")

                    def b_epi_scalar(c):
                        m = c % 2
                        p.op("vector",
                             lambda e, m=m: e.reciprocal(lrec[:, m, :], psD[m]),
                             reads=[B(f"bkDR{m}")], writes=[B(f"lrec{m}")])
                        p.op("scalar",
                             lambda e, m=m: e.copy(lrhi[:, m, :], lrec[:, m, :]),
                             reads=[B(f"lrec{m}")], writes=[B(f"lrhi{m}")])
                        p.op("vector",
                             lambda e, m=m: e.tensor_sub(lrlo[:, m, :],
                                                         lrec[:, m, :],
                                                         lrhi[:, m, :]),
                             reads=[B(f"lrec{m}"), B(f"lrhi{m}")],
                             writes=[B(f"lrlo{m}")])

                    def b_epi_pe(h, c):
                        m = c % 2
                        p.op("tensor",
                             lambda e, m=m: e.matmul(psR[m][:], onesr[:],
                                                     lrhi[:, m, :],
                                                     start=True, stop=False),
                             reads=[B("onesr"), B(f"lrhi{m}")],
                             writes=[B(f"bkDR{m}")])
                        p.op("tensor",
                             lambda e, m=m: e.matmul(psR[m][:], onesr[:],
                                                     lrlo[:, m, :],
                                                     start=False, stop=True),
                             reads=[B("onesr"), B(f"lrlo{m}")],
                             writes=[B(f"bkDR{m}")])
                        p.op("vector",
                             lambda e, m=m: e.tensor_copy(reps[:, m, :], psR[m][:]),
                             reads=[B(f"bkDR{m}")], writes=[B(f"reps{m}")])
                        p.op("vector",
                             lambda e, h=h, c=c, m=m: e.tensor_mul(
                                 OT[:, h, c * 512:(c + 1) * 512], psO[m][:],
                                 reps[:, m, :]),
                             reads=[B(f"psO{m}"), B(f"reps{m}")],
                             writes=[B("OT")], tag="oev")

                    LAG = 4
                    head_load(0)
                    pend = [None]
                    for h in range(H):
                        if h + 1 < H:
                            head_load(h + 1)
                        rd = h % 2
                        for c in range(4):
                            b_scores(h, c, 0)
                            for g in range(8):
                                sr = g % 2
                                pr = g % 5
                                if g + 1 < 8:
                                    b_scores(h, c, g + 1)
                                p.op("scalar",
                                     lambda e, sr=sr, pr=pr, rd=rd: e.activation(
                                         pt[:, pr, :], psS[sr][:], AF.Exp,
                                         bias=negshift[:, rd:rd + 1], scale=1.0),
                                     reads=[B(f"psS{sr}"), B(f"sh{rd}")],
                                     writes=[B(f"pt{pr}"), B(f"psS{sr}")], tag="exp")
                                if g == 0 and c == 3:
                                    b_shift_chain(h)
                                if g >= LAG:
                                    b_pv_dsum(h, c, g - LAG)
                                if g == 6 and pend[0] is not None:
                                    b_epi_pe(*pend[0])
                                    pend[0] = None
                            for g in range(8 - LAG, 8):
                                b_pv_dsum(h, c, g)
                            b_epi_scalar(c)
                            pend[0] = (h, c)
                    b_epi_pe(*pend[0])
                    pend[0] = None
                    p.barrier(f"bb{b}")

                if "C" not in PH:
                    continue
                with ExitStack() as ec:
                    wob = ec.enter_context(nc.sbuf_tensor(f"c_wo{b}", [128, 2, NCK, 128], BF16))
                    hbuf = ec.enter_context(nc.sbuf_tensor(f"c_h{b}", [128, NCK, 512], F32))
                    xcf, hcv, x2st, x1tst = cxcf, chcv, cx2st, cx1tst
                    psY = [ec.enter_context(
                        nc.psum_tensor(f"c_psY{b}{r}", [128, 512], F32))
                        for r in range(2)]
                    psSt = [ec.enter_context(
                        nc.psum_tensor(f"c_psSt{b}{r}", [128, 512], F32))
                        for r in range(2)]
                    psRp = [ec.enter_context(
                        nc.psum_tensor(f"c_psRp{b}{r}", [128, 512], F32))
                        for r in range(2)]

                    def wo_load(c, j):
                        rr = wot_b[:, j * 128:(j + 1) * 128].rearrange(
                            "(c p) n -> p c n", p=128)
                        p.dma("sync", f"c_w{j % 2}",
                              lambda e, rr=rr, j=j: e.dma_start(
                                  out=wob[:, j % 2, :, :], in_=rr),
                              reads=[B("wot_b"), B(f"wob{j % 2}")],
                              writes=[B(f"wobw{j % 2}")])

                    def xc_load(c, j):
                        p.dma("sync", f"c_x{j % 2}",
                              lambda e, c=c, j=j, b=b: e.dma_start(
                                  out=xcf[:, j % 2, :],
                                  in_=xtin[b, j * 128:(j + 1) * 128,
                                           c * 512:(c + 1) * 512]),
                              reads=[B(f"xcf{j % 2}")], writes=[B(f"xcfw{j % 2}")])

                    def emit_stats_c(j):
                        p.op("tensor",
                             lambda e, j=j: e.matmul(
                                 psSt[0][0:1, :], onesc[:], hcv[:, j % 3, 0, :],
                                 start=(j == 0), stop=(j == NCK - 1)),
                             reads=[B("onesc"), B(f"hcv{j % 3}")],
                             writes=[B("psSt0")], tag="stm")
                        p.op("tensor",
                             lambda e, j=j: e.matmul(
                                 psSt[1][0:1, :], onesc[:], hcv[:, j % 3, 1, :],
                                 start=(j == 0), stop=(j == NCK - 1)),
                             reads=[B("onesc"), B(f"hcq{j % 3}")],
                             writes=[B("psSt1")], tag="stm")

                    for c in range(4):
                        csl = slice(c * 512, (c + 1) * 512)
                        wo_load(c, 0)
                        xc_load(c, 0)
                        for j in range(NCK):
                            yr = j % 2
                            if j + 1 < NCK:
                                wo_load(c, j + 1)
                                xc_load(c, j + 1)
                            for ck in range(NCK):
                                p.op("tensor",
                                     lambda e, ck=ck, j=j, csl=csl, yr=yr: e.matmul(
                                         psY[yr][:], wob[:, j % 2, ck, :],
                                         OT[:, ck, csl],
                                         start=(ck == 0), stop=(ck == NCK - 1)),
                                     reads=[B(f"wobw{j % 2}"), B("OT")],
                                     writes=[B(f"psY{yr}")], tag="yomm")
                            p.op("vector",
                                 lambda e, j=j, yr=yr: e.scalar_tensor_tensor(
                                     hbuf[:, j, :], psY[yr][:], s_ob[:, j:j + 1],
                                     xcf[:, j % 2, :], ALU.add, ALU.add),
                                 reads=[B(f"psY{yr}"), B("ob"), B(f"xcfw{j % 2}")],
                                 writes=[B("hbuf"), B(f"psY{yr}"), B(f"xcf{j % 2}")],
                                 tag="resid")
                            p.op("scalar",
                                 lambda e, j=j: e.copy(hcv[:, j % 3, 0, :],
                                                       hbuf[:, j, :]),
                                 reads=[B("hbuf")], writes=[B(f"hcv{j % 3}")],
                                 tag="hcv")
                            p.op("scalar",
                                 lambda e, j=j: e.activation(hcv[:, j % 3, 1, :],
                                                             hbuf[:, j, :],
                                                             AF.Square),
                                 reads=[B("hbuf")], writes=[B(f"hcq{j % 3}")],
                                 tag="hsq")
                            if j >= 2:
                                emit_stats_c(j - 2)
                        for jl in range(NCK - 2, NCK):
                            emit_stats_c(jl)
                        p.op("vector",
                             lambda e: e.tensor_scalar_mul(rows[:, 0, :],
                                                           psSt[0][0:1, :],
                                                           1.0 / D),
                             reads=[B("psSt0")], writes=[B("ln_mu")], tag="mu")
                        p.op("vector",
                             lambda e: e.tensor_scalar_mul(rows[:, 1, :],
                                                           psSt[1][0:1, :],
                                                           1.0 / D),
                             reads=[B("psSt1")], writes=[B("ln_msq"), B("psSt1")])
                        p.op("vector",
                             lambda e: e.scalar_tensor_tensor(
                                 rows[:, 2, :], rows[:, 0, :], 1.0, rows[:, 0, :],
                                 ALU.mult, ALU.mult),
                             reads=[B("ln_mu")], writes=[B("ln_mu2")])
                        p.op("vector",
                             lambda e: e.tensor_sub(rows[:, 1, :], rows[:, 1, :],
                                                    rows[:, 2, :]),
                             reads=[B("ln_msq"), B("ln_mu2")], writes=[B("ln_var")])
                        p.op("scalar",
                             lambda e: e.activation(rows[:, 2, :], rows[:, 1, :],
                                                    AF.Sqrt, bias=s_eps[:]),
                             reads=[B("ln_var"), B("eps")], writes=[B("ln_std")])
                        p.op("vector",
                             lambda e: e.reciprocal(rows[:, 3, :], rows[:, 2, :]),
                             reads=[B("ln_std")], writes=[B("ln_rstd")])
                        p.op("vector",
                             lambda e: e.tensor_scalar_mul(rows[:, 3, :],
                                                           rows[:, 3, :], 2.0),
                             reads=[B("ln_rstd")], writes=[B("ln_r2_row")])
                        p.op("vector",
                             lambda e: e.scalar_tensor_tensor(
                                 rows[:, 4, :], rows[:, 0, :], -1.0, rows[:, 3, :],
                                 ALU.mult, ALU.mult),
                             reads=[B("ln_mu"), B("ln_r2_row")],
                             writes=[B("ln_n2_row")])
                        rep_row(3, 0, psRp[0][:], lrepA[:], B("lnrepA"), "ln_r2")
                        rep_row(4, 2, psRp[1][:], lrepB[:], B("lnrepB"), "ln_n2")
                        for j in range(NCK):
                            jr = j % 2
                            p.op("vector",
                                 lambda e, j=j, jr=jr: e.tensor_mul(
                                     x2st[:, jr, :], hbuf[:, j, :], lrepA[:]),
                                 reads=[B("hbuf"), B("lnrepA"), B(f"x2st{jr}")],
                                 writes=[B(f"x2w{jr}")], tag="nrm1")
                            p.op("vector",
                                 lambda e, jr=jr: e.tensor_add(
                                     x2st[:, jr, :], x2st[:, jr, :], lrepB[:]),
                                 reads=[B(f"x2w{jr}"), B("lnrepB")],
                                 writes=[B(f"x2w{jr}")], tag="nrm2")
                            p.op("scalar",
                                 lambda e, jr=jr: e.activation(
                                     x1tst[:, jr, :], x2st[:, jr, :], AF.Copy,
                                     scale=0.5),
                                 reads=[B(f"x2w{jr}")], writes=[B(f"x1tw{jr}")],
                                 tag="x1cv")
                            p.dma("scalar", f"c_spx{jr}",
                                  lambda e, j=j, jr=jr, csl=csl, b=b: e.dma_start(
                                      out=x1_s[b, j][:, csl], in_=x2st[:, jr, :]),
                                  reads=[B(f"x2w{jr}")],
                                  writes=[B(f"x2st{jr}"), B(f"x1spill_{b}_{j}_{c}")])
                            p.dma("scalar", f"c_spt{jr}",
                                  lambda e, j=j, jr=jr, c=c, b=b: e.dma_start(
                                      out=x1t_b[j][:, b * S + c * 512:
                                                   b * S + (c + 1) * 512],
                                      in_=x1tst[:, jr, :]),
                                  reads=[B(f"x1tw{jr}")],
                                  writes=[B(f"x1tspill_{b}_{j}_{c}")])
                    p.barrier(f"cb{b}")

        with ExitStack() as ed:
            x1c = ed.enter_context(nc.sbuf_tensor("d_x1c", [128, NCK, 512], BF16))
            h1 = ed.enter_context(nc.sbuf_tensor("d_h1", [128, 64, 512], BF16))
            w1blk = ed.enter_context(nc.sbuf_tensor("d_w1", [128, 3, NCK, 128], BF16))
            w2blk = ed.enter_context(nc.sbuf_tensor("d_w2", [128, 2, 2, 32, 128], BF16))
            fbuf = ed.enter_context(nc.sbuf_tensor("d_f", [128, NCK, 512], F32))

            fcv = chcv
            x2ld = cxcf
            wrep = ed.enter_context(nc.sbuf_tensor("d_wrep", [128, 512], F32))
            drepA = lrepA
            drepB = lrepB
            outst = cx2st
            psH = [ed.enter_context(nc.psum_tensor(f"d_psH{r}", [128, 512], F32))
                   for r in range(2)]
            psY2 = [ed.enter_context(nc.psum_tensor(f"d_psY{r}", [128, 512], F32))
                    for r in range(2)]
            psSt2 = [ed.enter_context(nc.psum_tensor(f"d_psS{r}", [128, 512], F32))
                     for r in range(2)]
            psRp2 = [ed.enter_context(nc.psum_tensor(f"d_psR{r}", [128, 512], F32))
                     for r in range(2)]

            NQ = (TOK // 512) if "D" in PH else 0

            def d_loads(q):
                for ck in range(NCK):
                    p.dma("sync", "d_x1",
                          lambda e, ck=ck, q=q: e.dma_start(
                              out=x1c[:, ck, :],
                              in_=x1t_b[ck][:, q * 512:(q + 1) * 512]),
                          reads=[B(f"x1tspill_{q // 4}_{ck}_{q % 4}"), B("x1c")],
                          writes=[B("x1cw")], tag="dx1")

            def w1_load(q, jf):
                rr = w1_b[:, jf * 128:(jf + 1) * 128].rearrange(
                    "(c p) n -> p c n", p=128)
                p.dma("sync", f"d_w1{jf % 3}",
                      lambda e, rr=rr, jf=jf: e.dma_start(
                          out=w1blk[:, jf % 3, :, :], in_=rr),
                      reads=[B("w1_b"), B(f"w1blk{jf % 3}")],
                      writes=[B(f"w1w{jf % 3}")])

            def w2_load(q, j2):
                for hlf in range(2):
                    rr = w2_b[hlf * 4096:(hlf + 1) * 4096,
                              j2 * 128:(j2 + 1) * 128].rearrange(
                        "(c p) n -> p c n", p=128)
                    p.dma("sync", f"d_w2{j2 % 2}",
                          lambda e, rr=rr, j2=j2, hlf=hlf: e.dma_start(
                              out=w2blk[:, j2 % 2, hlf, :, :], in_=rr),
                          reads=[B("w2_b"), B(f"w2blk{j2 % 2}")],
                          writes=[B(f"w2w{j2 % 2}")])

            def x2_load(q, j2):
                bq, cq = q // 4, q % 4
                p.dma("sync", f"d_x2{j2 % 2}",
                      lambda e, j2=j2, bq=bq, cq=cq: e.dma_start(
                          out=x2ld[:, j2 % 2, :],
                          in_=x1_s[bq, j2][:, cq * 512:(cq + 1) * 512]),
                      reads=[B(f"x1spill_{bq}_{j2}_{cq}"), B(f"x2ld{j2 % 2}")],
                      writes=[B(f"x2w{j2 % 2}")])

            for q in range(NQ):
                d_loads(q)
                for ck in range(NCK):
                    p.op("tensor",
                         lambda e, ck=ck: e.matmul(
                             psSt2[0][0:1, :], s_rw[:, ck:ck + 1], x1c[:, ck, :],
                             start=(ck == 0), stop=(ck == NCK - 1)),
                         reads=[B("rw"), B("x1cw")], writes=[B("psSt20")],
                         tag="rt")
                p.op("scalar",
                     lambda e: e.activation(rows[:, 5, :], psSt2[0][0:1, :],
                                            AF.Tanh, bias=s_rbh[:], scale=0.5),
                     reads=[B("psSt20"), B("rbh")], writes=[B("rt_t")])
                p.op("vector",
                     lambda e: e.tensor_scalar(rows[:, 6, :], rows[:, 5, :],
                                               0.5, 0.5, ALU.mult, ALU.add),
                     reads=[B("rt_t")], writes=[B("rt_w_row")])
                rep_row(6, 4, psRp2[0][:], wrep[:], B("wrep"), "rt_w")

                w1_load(q, 0)
                w1_load(q, 1)
                for jf in range(64):
                    hr = jf % 2
                    if jf + 2 < 64:
                        w1_load(q, jf + 2)
                    for ck in range(NCK):
                        p.op("tensor",
                             lambda e, ck=ck, jf=jf, hr=hr: e.matmul(
                                 psH[hr][:], w1blk[:, jf % 3, ck, :],
                                 x1c[:, ck, :],
                                 start=(ck == 0), stop=(ck == NCK - 1)),
                             reads=[B(f"w1w{jf % 3}"), B("x1cw")],
                             writes=[B(f"psH{hr}")], tag="f1mm")
                    p.op("scalar",
                         lambda e, jf=jf, hr=hr: e.activation(
                             h1[:, jf, :], psH[hr][:], AF.Gelu_apprx_tanh,
                             bias=s_b1[:, jf:jf + 1], scale=1.0),
                         reads=[B(f"psH{hr}"), B("b1")],
                         writes=[B(f"h1_{jf}"), B(f"psH{hr}")], tag="gelu")

                def emit_stats_d(j2):
                    p.op("tensor",
                         lambda e, j2=j2: e.matmul(
                             psSt2[0][32:33, :], onesc[:], fcv[:, j2 % 3, 0, :],
                             start=(j2 == 0), stop=(j2 == NCK - 1)),
                         reads=[B("onesc"), B(f"fcv{j2 % 3}")],
                         writes=[B("psSt20")], tag="stm2")
                    p.op("tensor",
                         lambda e, j2=j2: e.matmul(
                             psSt2[1][0:1, :], onesc[:], fcv[:, j2 % 3, 1, :],
                             start=(j2 == 0), stop=(j2 == NCK - 1)),
                         reads=[B("onesc"), B(f"fcq{j2 % 3}")],
                         writes=[B("psSt21")], tag="stm2")

                w2_load(q, 0)
                x2_load(q, 0)
                for j2 in range(NCK):
                    yr = j2 % 2
                    if j2 + 1 < NCK:
                        w2_load(q, j2 + 1)
                        x2_load(q, j2 + 1)
                    for jf in range(64):
                        p.op("tensor",
                             lambda e, jf=jf, j2=j2, yr=yr: e.matmul(
                                 psY2[yr][:],
                                 w2blk[:, j2 % 2, jf // 32, jf % 32, :],
                                 h1[:, jf, :],
                                 start=(jf == 0), stop=(jf == 63)),
                             reads=[B(f"w2w{j2 % 2}"), B(f"h1_{jf}")],
                             writes=[B(f"psY2{yr}")], tag="f2mm")
                    p.op("vector",
                         lambda e, j2=j2, yr=yr: e.scalar_tensor_tensor(
                             fbuf[:, j2, :], psY2[yr][:], s_b2[:, j2:j2 + 1],
                             wrep[:], ALU.add, ALU.mult),
                         reads=[B(f"psY2{yr}"), B("b2"), B("wrep")],
                         writes=[B("fbuf"), B(f"psY2{yr}")], tag="fep1")
                    p.op("vector",
                         lambda e, j2=j2, yr=yr: e.tensor_add(
                             fbuf[:, j2, :], fbuf[:, j2, :], x2ld[:, yr, :]),
                         reads=[B("fbuf"), B(f"x2w{yr}")],
                         writes=[B("fbuf"), B(f"x2ld{yr}")], tag="fep2")
                    p.op("scalar",
                         lambda e, j2=j2: e.copy(fcv[:, j2 % 3, 0, :],
                                                 fbuf[:, j2, :]),
                         reads=[B("fbuf")], writes=[B(f"fcv{j2 % 3}")])
                    p.op("scalar",
                         lambda e, j2=j2: e.activation(fcv[:, j2 % 3, 1, :],
                                                       fbuf[:, j2, :], AF.Square),
                         reads=[B("fbuf")], writes=[B(f"fcq{j2 % 3}")])
                    if j2 >= 2:
                        emit_stats_d(j2 - 2)
                for jl in range(NCK - 2, NCK):
                    emit_stats_d(jl)
                p.op("vector",
                     lambda e: e.tensor_scalar_mul(rows[:, 0, :],
                                                   psSt2[0][32:33, :], 1.0 / D),
                     reads=[B("psSt20")], writes=[B("ln_mu")])
                p.op("vector",
                     lambda e: e.tensor_scalar_mul(rows[:, 1, :],
                                                   psSt2[1][0:1, :], 1.0 / D),
                     reads=[B("psSt21")], writes=[B("ln_msq"), B("psSt21")])
                p.op("vector",
                     lambda e: e.scalar_tensor_tensor(
                         rows[:, 2, :], rows[:, 0, :], 1.0, rows[:, 0, :],
                         ALU.mult, ALU.mult),
                     reads=[B("ln_mu")], writes=[B("ln_mu2")])
                p.op("vector",
                     lambda e: e.tensor_sub(rows[:, 1, :], rows[:, 1, :],
                                            rows[:, 2, :]),
                     reads=[B("ln_msq"), B("ln_mu2")], writes=[B("ln_var")])
                p.op("scalar",
                     lambda e: e.activation(rows[:, 2, :], rows[:, 1, :],
                                            AF.Sqrt, bias=s_eps[:]),
                     reads=[B("ln_var"), B("eps")], writes=[B("ln_std")])
                p.op("vector",
                     lambda e: e.reciprocal(rows[:, 3, :], rows[:, 2, :]),
                     reads=[B("ln_std")], writes=[B("ln_r2_row")])
                p.op("vector",
                     lambda e: e.scalar_tensor_tensor(
                         rows[:, 4, :], rows[:, 0, :], -1.0, rows[:, 3, :],
                         ALU.mult, ALU.mult),
                     reads=[B("ln_mu"), B("ln_r2_row")], writes=[B("ln_n2_row")])
                rep_row(3, 0, psRp2[0][:], drepA[:], B("drepA"), "ln_r2")
                rep_row(4, 2, psRp2[1][:], drepB[:], B("drepB"), "ln_n2")
                bq, cq = q // 4, q % 4
                for j2 in range(NCK):
                    jr = j2 % 2
                    p.op("vector",
                         lambda e, j2=j2, jr=jr: e.tensor_mul(
                             outst[:, jr, :], fbuf[:, j2, :], drepA[:]),
                         reads=[B("fbuf"), B("drepA"), B(f"outst{jr}")],
                         writes=[B(f"outw{jr}")], tag="fin1")
                    p.op("vector",
                         lambda e, jr=jr: e.tensor_add(
                             outst[:, jr, :], outst[:, jr, :], drepB[:]),
                         reads=[B(f"outw{jr}"), B("drepB")],
                         writes=[B(f"outw{jr}")], tag="fin2")
                    p.dma("scalar", f"d_out{jr}",
                          lambda e, j2=j2, jr=jr, bq=bq, cq=cq: e.dma_start(
                              out=yt[bq, j2 * 128:(j2 + 1) * 128,
                                     cq * 512:(cq + 1) * 512],
                              in_=outst[:, jr, :]),
                          reads=[B(f"outw{jr}")],
                          writes=[B(f"outst{jr}"), B("ytout")], tag="ostore")

            for fs in ("d_out0", "d_out1"):
                fin = p.dma_sem(fs)
                p.op("sync", lambda e, fin=fin: e.wait_ge(fin.handle, fin.count))

        p.plan()
        p.emit(nc, es)
    return nc, p



_cached = {}


def _rope_tables():
    half = DH // 2
    inv = 1.0 / (10000.0 ** (np.arange(0, DH, 2, dtype=np.float64) / DH))
    t = np.arange(S, dtype=np.float64)
    fr = np.outer(inv, t)
    cos = np.cos(fr)
    sin = np.sin(fr)
    cosf = np.concatenate([cos, cos], 0)
    sinf = np.concatenate([-sin, sin], 0)
    return (np.ascontiguousarray(cosf.astype(np.float32).astype(BF)),
            np.ascontiguousarray(sinf.astype(np.float32).astype(BF)))


def kernel(L0_attn_x, L0_attn_q_w, L0_attn_k_w, L0_attn_v_w, L0_attn_q_b,
           L0_attn_k_b, L0_attn_v_b, L0_attn_out_w, L0_attn_out_b,
           L0_ffn_router_weight, L0_ffn_router_bias, L0_ffn_w1, L0_ffn_b1,
           L0_ffn_w2, L0_ffn_b2):
    n_cores = 8
    if "nc" not in _cached:
        _cached["nc"] = build_program()
    nc, _ = _cached["nc"]

    cosk_t, sink_t = _rope_tables()

    f32 = np.float32
    wqt = np.ascontiguousarray(np.asarray(L0_attn_q_w, f32).T)
    wkt = np.ascontiguousarray(np.asarray(L0_attn_k_w, f32).T)
    wvt = np.ascontiguousarray(np.asarray(L0_attn_v_w, f32).T)
    wot = np.ascontiguousarray(np.asarray(L0_attn_out_w, f32).T)
    w1 = np.ascontiguousarray(np.asarray(L0_ffn_w1, f32))
    w2 = np.ascontiguousarray(np.asarray(L0_ffn_w2, f32))
    qb2 = np.ascontiguousarray(np.asarray(L0_attn_q_b, f32).reshape(NCK, 128).T)
    kb2 = np.ascontiguousarray(np.asarray(L0_attn_k_b, f32).reshape(NCK, 128).T)
    vbr = np.ascontiguousarray(np.broadcast_to(np.asarray(L0_attn_v_b, f32).reshape(1, D), (128, D)))
    ob2 = np.ascontiguousarray(np.asarray(L0_attn_out_b, f32).reshape(NCK, 128).T)
    b1c = np.ascontiguousarray(np.asarray(L0_ffn_b1, f32).reshape(64, 128).T)
    b2c = np.ascontiguousarray(np.asarray(L0_ffn_b2, f32).reshape(NCK, 128).T)
    rwc = np.ascontiguousarray(
        np.asarray(L0_ffn_router_weight, f32).reshape(NCK, 128).T)
    rbx = np.asarray(L0_ffn_router_bias, f32).reshape(1, 1)
    idf = np.eye(128, dtype=f32)
    onesc = np.ones((128, 1), BF)
    onesr = np.ones((1, 128), BF)
    ones1f = np.ones((1, 128), f32)

    x = np.asarray(L0_attn_x, f32)
    in_maps = []
    for core in range(n_cores):
        xs = x[core * BL:(core + 1) * BL]
        xtin = np.ascontiguousarray(xs.transpose(0, 2, 1))
        in_maps.append(dict(
            xtin=xtin, wqt=wqt, wkt=wkt, wvt=wvt, wot=wot, w1=w1, w2=w2,
            qb2=qb2, kb2=kb2, vbr=vbr, ob2=ob2, b1c=b1c, b2c=b2c, rwc=rwc,
            rbx=rbx, cosk=cosk_t, sink=sink_t,
            idf=idf, onesc=onesc, onesr=onesr, ones1f=ones1f,
        ))

    res = run_bass_kernel_spmd(nc, in_maps, core_ids=list(range(n_cores)))
    out = np.empty((16, S, D), np.float32)
    for core in range(n_cores):
        ytc = res.results[core]["yt"]
        out[core * BL:(core + 1) * BL] = ytc.transpose(0, 2, 1)
    return out
